# version 21
# baseline (speedup 1.0000x reference)
import contextlib
import numpy as np
import ml_dtypes
import concourse.bass as bass
import concourse.mybir as mybir

F32 = mybir.dt.float32
BF16 = mybir.dt.bfloat16
U8 = mybir.dt.uint8
U32 = mybir.dt.uint32
AF = mybir.ActivationFunctionType
ALU = mybir.AluOpType
AX = mybir.AxisListType

D, E, NS, DTR, I, T = 1024, 2048, 16, 64, 2816, 2048
ET, TT = E // 128, T // 128
RM = float(1.5 * 2 ** 23)
CH = 512
NCH = T // CH
LIM_TT = 16
LIM_W = 100
LIM_CH = 4
LIM_ET = 16
P2_LEVEL = 4


_WRAP_OPS = {
    "tensor_tensor", "tensor_scalar", "tensor_copy", "tensor_reduce",
    "tensor_tensor_scan", "scalar_tensor_tensor", "copy_predicated",
    "tensor_tensor_reduce",
    "reciprocal", "memset", "select", "bn_stats", "bn_aggr", "iota",
    "tensor_add", "tensor_mul", "tensor_sub", "tensor_max", "tensor_relu",
    "tensor_scalar_add", "tensor_scalar_mul", "tensor_scalar_max",
    "tensor_scalar_min", "tensor_scalar_sub", "reduce_sum", "reduce_max",
    "activation", "copy", "add", "mul", "sign", "sqrt", "square",
    "stream_shuffle", "pool", "pool_avg", "pool_max",
}


class EngProxy:

    def __init__(self, kb, eng, sem):
        self._kb, self._eng, self._sem = kb, eng, sem

    def __getattr__(self, name):
        attr = getattr(self._eng, name)
        if name not in _WRAP_OPS:
            return attr
        kb, sem, eng = self._kb, self._sem, self._eng

        def wrapped(*a, **k):
            if kb.cnt[sem] > 0:
                eng.wait_ge(kb.sems[sem], kb.cnt[sem])
            instr = attr(*a, **k)
            instr.then_inc(kb.sems[sem], 1)
            kb.cnt[sem] += 1
            return instr
        return wrapped



class SBatch:

    def __init__(self, kb):
        self.kb = kb
        self.items = []

    def wait(self, sem, val=None):
        v = self.kb.cnt[sem] if val is None else val
        if v > 0:
            self.items.append(("wait", sem, v))

    def act(self, out, in_, func, scale=None, bias=None, accum_out=None):
        kb = self.kb
        kb.cnt["S"] += 1
        my = kb.cnt["S"]
        self.items.append(("act", my, out, in_, func, scale, bias, accum_out))
        return my

    def flush(self, B):
        if not self.items:
            return
        kb, items = self.kb, self.items
        self.items = []

        @B.scalar
        def _(scalar):
            for it in items:
                if it[0] == "wait":
                    _, sem, v = it
                    scalar.wait_ge(kb.sems[sem], v)
                else:
                    _, my, out, in_, func, scale, bias, accum_out = it
                    if my > 1:
                        scalar.wait_ge(kb.sems["S"], my - 1)
                    kw = {}
                    if scale is not None:
                        kw["scale"] = scale
                    if bias is not None:
                        kw["bias"] = bias
                    if accum_out is not None:
                        kw["accum_out"] = accum_out
                    scalar.activation(out=out, in_=in_, func=func, **kw).then_inc(kb.sems["S"], 1)



class KB:
    def __init__(self, nc):
        self.nc = nc
        self.B = None
        self.sems = {}
        self.cnt = {}
        for name in ["L", "L2", "St", "Sg", "V", "S", "Tn", "Lg", "Stg"]:
            self.sems[name] = nc.alloc_semaphore("sem_" + name)
            self.cnt[name] = 0

    def wrap(self, eng, sem):
        return EngProxy(self, eng, sem)

    def inc(self, instr, sem, n=1):
        if sem in ("V", "S", "G"):
            return self.cnt[sem]
        instr.then_inc(self.sems[sem], n)
        self.cnt[sem] += n
        return self.cnt[sem]

    def dma(self, eng, out, in_, sem="L"):
        if self.cnt[sem] > 0:
            eng.wait_ge(self.sems[sem], self.cnt[sem])
        return self.inc(eng.dma_start(out=out, in_=in_), sem, 16)

    def wait(self, eng, sem, val=None):
        v = self.cnt[sem] if val is None else val
        if v > 0:
            eng.wait_ge(self.sems[sem], v)


def emit_quant(kb, vector, x_ap, out_bf_ap, bufs, F):
    NB = F // 16
    bm, z, t1, qb, m8 = bufs["bm"], bufs["z"], bufs["t1"], bufs["qb"], bufs["m8"]
    x3 = x_ap.rearrange("p (m k) -> p m k", k=16)
    vector.tensor_reduce(bm[:, 0:NB], x3, axis=AX.X, op=ALU.max, apply_absolute_value=True)
    vector.tensor_scalar(bm[:, NB:2 * NB], bm[:, 0:NB], 6e-12, None, ALU.max)
    vector.reciprocal(bm[:, 2 * NB:3 * NB], bm[:, NB:2 * NB])
    rb = bm[:, 2 * NB:3 * NB].unsqueeze(-1).broadcast_to((128, NB, 16))
    vector.scalar_tensor_tensor(z[:, :F].rearrange("p (m k) -> p m k", k=16),
                                x3, 6.0, rb, ALU.mult, ALU.mult)
    vector.tensor_scalar(t1[:, :F], z[:, :F], 2.0, RM, ALU.mult, ALU.add)
    vector.tensor_scalar(t1[:, :F], t1[:, :F], -RM, 0.5, ALU.add, ALU.mult)
    vector.tensor_scalar(qb[:, :F].bitcast(U32), z[:, :F].bitcast(U32),
                         0x00200000, None, ALU.add)
    vector.tensor_scalar(qb[:, :F].bitcast(U32), qb[:, :F].bitcast(U32),
                         0xFFC00000, None, ALU.bitwise_and)
    vector.tensor_scalar(z[:, :F].bitcast(U32), z[:, :F].bitcast(U32),
                         0x7FFFFFFF, None, ALU.bitwise_and)
    vector.tensor_scalar(m8[:, :F], z[:, :F], 2.0, None, ALU.is_ge)
    vector.copy_predicated(t1[:, :F], m8[:, :F], qb[:, :F])
    vector.tensor_scalar(bm[:, NB:2 * NB], bm[:, NB:2 * NB], 1.0 / 6.0, None, ALU.mult)
    rb2 = bm[:, NB:2 * NB].unsqueeze(-1).broadcast_to((128, NB, 16))
    return vector.tensor_tensor(out_bf_ap.rearrange("p (m k) -> p m k", k=16),
                                t1[:, :F].rearrange("p (m k) -> p m k", k=16),
                                rb2, ALU.mult)


def quant_bufs(st, nc, pre, F):
    sb = lambda n, s, d=F32: st.enter_context(nc.sbuf_tensor(n, s, d))
    return {
        "bm": sb(pre + "_bm", [128, 3 * (F // 16)]),
        "z": sb(pre + "_z", [128, F]),
        "t1": sb(pre + "_t1", [128, F]),
        "qb": sb(pre + "_qb", [128, F]),
        "m8": sb(pre + "_m8", [128, F], U8),
    }


def build_nc(upto="full", dbg=None):
    nc = bass.Bass("TRN2", target_bir_lowering=False)
    kb = KB(nc)

    dr = lambda n, s, dt=F32, kind="ExternalInput": nc.dram_tensor(n, s, dt, kind=kind).ap()
    x_own = dr("x_own", [T, D])
    x_pre = dr("x_pre", [T, D])
    maskin = dr("maskin", [1, 1])
    normw_c = dr("normw_c", [128, 8])
    inwT = dr("inwT", [D, E])
    skipwT = dr("skipwT", [D, E])
    outwT = dr("outwT", [E, D])
    wd1T_c = dr("wd1T_c", [128, ET * DTR])
    wd2T = dr("wd2T", [DTR, E])
    wbT_c = dr("wbT_c", [128, ET * NS])
    wcT_c = dr("wcT_c", [128, ET * NS])
    wd2b_c = dr("wd2b_c", [128, ET])
    convb_c = dr("convb_c", [128, ET])
    convw_c = dr("convw_c", [128, ET * 4])
    Alog_c = dr("Alog_c", [128, ET * NS])
    Wd_c = dr("Wd_c", [128, ET])
    wg = dr("wg", [I, D])
    wu = dr("wu", [I, D])
    wdn = dr("wdn", [D, I])
    eye_in = dr("eye_in", [128, 128], BF16)
    out = dr("out", [T, D], F32, kind="ExternalOutput")

    di = lambda n, s, dt=F32: nc.dram_tensor(n, s, dt, kind="Internal").ap()
    racc = di("racc", [T, D])
    sdelta = di("sdelta", [E, T])
    sxc = di("sxc", [E, T])
    sskip = di("sskip", [E, T])
    sym = di("sym", [E, T], BF16)
    sypart = di("sypart", [E, T])
    sB = di("sB", [NS, T], BF16)
    sC = di("sC", [NS, T], BF16)
    sffn = di("sffn", [T, D])
    swgT = di("swgT", [D, I], BF16)
    swuT = di("swuT", [D, I], BF16)
    swdT = di("swdT", [I, D], BF16)

    sbp = lambda n, s, dt=F32: nc.alloc_sbuf_tensor(n, s, dt)
    eye_sb = sbp("eye_sb", [128, 128], BF16)
    normw_sb = sbp("normw_sb", [128, 8])
    wd1T_sb = sbp("wd1T_sb", [128, ET * DTR], BF16)
    wd2T_sb = sbp("wd2T_sb", [DTR, E], BF16)
    wbT_sb = sbp("wbT_sb", [128, ET * NS], BF16)
    wcT_sb = sbp("wcT_sb", [128, ET * NS], BF16)
    wd2b_sb = sbp("wd2b_sb", [128, ET])
    convb_sb = sbp("convb_sb", [128, ET])
    convw_sb = sbp("convw_sb", [128, ET * 4])
    A_sb = sbp("A_sb", [128, ET * NS])
    nA_sb = sbp("nA_sb", [128, ET * NS])
    Wd_sb = sbp("Wd_sb", [128, ET])
    maskv = sbp("maskv", [128, 1])
    h0_sb = sbp("h0_sb", [128, ET * NS])
    halo_sb = sbp("halo_sb", [128, ET * 3])
    zero1 = sbp("zero1", [128, 1])
    stat = sbp("stat", [128, 8])

    st_xrA = contextlib.ExitStack()
    st_xrB = contextlib.ExitStack()
    xrT_own = st_xrA.enter_context(nc.sbuf_tensor("xrT_own", [128, 8, T], BF16))

    B = nc.Block()
    B.__enter__()
    kb.B = B

    with nc.sbuf_tensor("p0_tmp", [128, E], F32) as tmpw:
        @B.sync
        def _(sync):
            kb.dma(sync, eye_sb[:], eye_in)
            kb.dma(sync, normw_sb[:], normw_c)
            kb.dma(sync, wd2b_sb[:], wd2b_c)
            kb.dma(sync, convb_sb[:], convb_c)
            kb.dma(sync, convw_sb[:], convw_c)
            kb.dma(sync, A_sb[:], Alog_c)
            kb.dma(sync, Wd_sb[:], Wd_c)
            kb.dma(sync, maskv[:], maskin.broadcast_to((128, 1)))
            kb.dma(sync, tmpw[:, :ET * DTR], wd1T_c)
        ld0 = kb.cnt["L"]

        @B.scalar
        def _(scalar):
            scalar = kb.wrap(scalar, "S")
            kb.wait(scalar, "L", ld0)
            kb.inc(scalar.activation(out=A_sb[:], in_=A_sb[:], func=AF.Exp), "S")
        s_a = kb.cnt["S"]

        @B.vector
        def _(vector):
            vector = kb.wrap(vector, "V")
            kb.wait(vector, "L", ld0)
            vector.memset(zero1[:], 0.0)
            vector.memset(halo_sb[:], 0.0)
            vector.memset(h0_sb[:], 0.0)
            vector.tensor_copy(wd1T_sb[:], tmpw[:, :ET * DTR])
            kb.wait(vector, "S", s_a)
            vector.tensor_copy(nA_sb[:], A_sb[:])
            kb.inc(vector.tensor_scalar(A_sb[:], A_sb[:], -1.0, None, ALU.mult), "V")

        @B.sync
        def _(sync):
            kb.wait(sync, "V")
            kb.dma(sync, tmpw[:, :ET * NS], wbT_c)
            kb.dma(sync, tmpw[:, ET * NS:2 * ET * NS], wcT_c)
        ld1 = kb.cnt["L"]

        @B.vector
        def _(vector):
            vector = kb.wrap(vector, "V")
            kb.wait(vector, "L", ld1)
            vector.tensor_copy(wbT_sb[:], tmpw[:, :ET * NS])
            kb.inc(vector.tensor_copy(wcT_sb[:], tmpw[:, ET * NS:2 * ET * NS]), "V")

        @B.sync
        def _(sync):
            kb.wait(sync, "V")
            kb.dma(sync, tmpw[:DTR, :E], wd2T)
        ld2 = kb.cnt["L"]

        @B.vector
        def _(vector):
            vector = kb.wrap(vector, "V")
            kb.wait(vector, "L", ld2)
            kb.inc(vector.tensor_copy(wd2T_sb[:], tmpw[:DTR, :E]), "V")

    dbgmap = {"sdelta": (sdelta, F32), "sxc": (sxc, F32), "sskip": (sskip, F32),
              "sym": (sym, BF16), "sypart": (sypart, F32)}
    dbg_out = None
    if dbg is not None:
        dbg_out = nc.dram_tensor("dbg", [E, T], dbgmap[dbg][1], kind="ExternalOutput").ap()

    def finish():
        @B.sync
        def _(sync):
            kb.wait(sync, "St")
            kb.wait(sync, "V")
            kb.wait(sync, "S")
            kb.wait(sync, "Tn")
            kb.dma(sync, out, racc, sem="St")
            if dbg is not None:
                kb.dma(sync, dbg_out, dbgmap[dbg][0], sem="St")
            kb.wait(sync, "St")
        B.__exit__(None, None, None)
        return nc, kb

    if upto == "p0":
        return finish()

    def drive(*gens):
        live = [g for g, _ in gens]
        reps = {id(g): r for g, r in gens}
        while live:
            for g in list(live):
                for _ in range(reps[id(g)]):
                    try:
                        next(g)
                    except StopIteration:
                        live.remove(g)
                        break


    def gen_p0b():
      with contextlib.ExitStack() as st:
        qb0 = quant_bufs(st, nc, "q0", I)
        wtile = [st.enter_context(nc.sbuf_tensor(f"q0_w{j}", [128, I], F32)) for j in range(2)]
        wqbf = [st.enter_context(nc.sbuf_tensor(f"q0_wq{j}", [128, I], BF16)) for j in range(2)]
        wide = st.enter_context(nc.sbuf_tensor("q0_wide", [128, 22 * 128], BF16))
        qps = [st.enter_context(nc.psum_tensor(f"q0_ps{j}", [128, 128], BF16)) for j in range(2)]

        jobs = []
        for it in range(I // 128):
            jobs.append((wg, swgT, it, D))
        for it in range(I // 128):
            jobs.append((wu, swuT, it, D))
        for it in range(D // 128):
            jobs.append((wdn, swdT, it, I))

        q_done = [kb.cnt["V"], kb.cnt["V"]]
        tr_done = [kb.cnt["Tn"], kb.cnt["Tn"]]
        ld_done = [0, 0]
        st_done = kb.cnt["St"]
        for ji, (srcw, dst, it, cols) in enumerate(jobs):
            j = ji % 2
            lsem = "Lg"
            @B.gpsimd
            def _(gpsimd):
                kb.wait(gpsimd, "V", q_done[j])
                kb.dma(gpsimd, wtile[j][:, :cols], srcw[it * 128:(it + 1) * 128, :], sem=lsem)
            ld_done[j] = kb.cnt[lsem]

            @B.vector
            def _(vector):
                vector = kb.wrap(vector, "V")
                kb.wait(vector, lsem, ld_done[j])
                kb.wait(vector, "Tn", tr_done[j])
                emit_quant(kb, vector, wtile[j][:, :cols], wqbf[j][:, :cols], qb0, cols)
            q_done[j] = kb.cnt["V"]

            nblk = cols // 128
            sba = SBatch(kb)
            ev = 0
            for jt in range(nblk):
                pj = jt % 2
                @B.tensor
                def _(tensor):
                    kb.wait(tensor, "V", q_done[j])
                    kb.wait(tensor, "S")
                    kb.inc(tensor.transpose(qps[pj][:], wqbf[j][:, jt * 128:(jt + 1) * 128], eye_sb[:]), "Tn")
                tj = kb.cnt["Tn"]
                sba.wait("Tn", tj)
                if jt == 0:
                    sba.wait("St", st_done)
                ev = sba.act(wide[:, jt * 128:(jt + 1) * 128], qps[pj][:], AF.Copy)
            sba.flush(B)
            tr_done[j] = kb.cnt["Tn"]

            @B.sync
            def _(sync):
                kb.wait(sync, "S", ev)
                kb.dma(sync, dst.rearrange("(a p) k -> p a k", p=128)[:, :, it * 128:(it + 1) * 128],
                       wide[:].rearrange("p (a k) -> p a k", k=128)[:, :nblk, :], sem="St")
            st_done = kb.cnt["St"]
            yield

    if upto == "p0b":
        return finish()

    def gen_p1(xsrc, xrT_dst, own, xqT_dst):
        with contextlib.ExitStack() as st:
            sb = lambda n, s, dt=F32: st.enter_context(nc.sbuf_tensor(n, s, dt))
            pre = "p1o" if own else "p1p"
            xb = sb(pre + "_x", [128, D])
            xnb = sb(pre + "_xn", [128, D])
            scr = sb(pre + "_scr", [128, D])
            zb = sb(pre + "_zb", [128, D], BF16)
            trp = [st.enter_context(nc.psum_tensor(pre + f"_ps{j}", [128, 128], BF16)) for j in range(2)]
            if own:
                rb = sb(pre + "_r", [128, D])
                qbf = sb(pre + "_qbf", [128, D], BF16)
                qb1 = quant_bufs(st, nc, pre + "q", D)

            for tt in range(min(TT, LIM_TT)):
                sba = SBatch(kb)
                @B.sync
                def _(sync):
                    kb.wait(sync, "St")
                    kb.wait(sync, "V")
                    kb.wait(sync, "S")
                    kb.dma(sync, xb[:], xsrc[tt * 128:(tt + 1) * 128, :])
                ldv = kb.cnt["L"]

                sba.wait("L", ldv)
                s_sq = sba.act(scr[:], xb[:], AF.Square, accum_out=stat[:, 0:1])

                @B.vector
                def _(vector):
                    vector = kb.wrap(vector, "V")
                    kb.wait(vector, "L", ldv)
                    vector.tensor_reduce(stat[:, 1:2], xb[:], axis=AX.X, op=ALU.add)
                    vector.tensor_scalar(stat[:, 2:3], stat[:, 1:2], 1.0 / D, None, ALU.mult)
                    kb.wait(vector, "S", s_sq)
                    vector.tensor_scalar(stat[:, 4:5], stat[:, 0:1], 1.0 / D, 1e-5, ALU.mult, ALU.add)
                    vector.tensor_tensor(stat[:, 5:6], stat[:, 2:3], stat[:, 2:3], ALU.mult)
                    vector.tensor_tensor(stat[:, 4:5], stat[:, 4:5], stat[:, 5:6], ALU.subtract)
                v_rv = kb.cnt["V"]

                sba.wait("V", v_rv)
                sba.act(stat[:, 5:6], stat[:, 4:5], AF.Ln)
                s_rstd = sba.act(stat[:, 6:7], stat[:, 5:6], AF.Exp, scale=-0.5)

                @B.vector
                def _(vector):
                    vector = kb.wrap(vector, "V")
                    kb.wait(vector, "S", s_rstd)
                    vector.tensor_scalar(xnb[:], xb[:], stat[:, 2:3], stat[:, 6:7],
                                         ALU.subtract, ALU.mult)
                v_xn = kb.cnt["V"]

                if own:
                    @B.vector
                    def _(vector):
                        vector = kb.wrap(vector, "V")
                        vector.tensor_tensor(rb[:], xb[:], xnb[:], ALU.add)
                    v_r = kb.cnt["V"]

                    @B.sync
                    def _(sync):
                        kb.wait(sync, "V", v_r)
                        kb.dma(sync, racc[tt * 128:(tt + 1) * 128, :], rb[:], sem="St")

                    @B.vector
                    def _(vector):
                        vector = kb.wrap(vector, "V")
                        emit_quant(kb, vector, xnb[:], qbf[:], qb1, D)
                    v_q = kb.cnt["V"]

                    for dt in range(8):
                        @B.tensor
                        def _(tensor):
                            kb.wait(tensor, "V")
                            kb.inc(tensor.transpose(trp[dt % 2][:], qbf[:, dt * 128:(dt + 1) * 128], eye_sb[:]), "Tn")

                        @B.vector
                        def _(vector):
                            vector = kb.wrap(vector, "V")
                            kb.wait(vector, "Tn")
                            vector.tensor_copy(xqT_dst[:, dt, tt * 128:(tt + 1) * 128], trp[dt % 2][:])

                sba.wait("V", v_xn)
                s_sq2 = sba.act(scr[:], xnb[:], AF.Square, accum_out=stat[:, 0:1])

                @B.vector
                def _(vector):
                    vector = kb.wrap(vector, "V")
                    kb.wait(vector, "S", s_sq2)
                    vector.tensor_scalar(stat[:, 4:5], stat[:, 0:1], 1.0 / D, 1e-5, ALU.mult, ALU.add)
                v_rr = kb.cnt["V"]

                sba.wait("V", v_rr)
                sba.act(stat[:, 5:6], stat[:, 4:5], AF.Ln)
                s_rms = sba.act(stat[:, 6:7], stat[:, 5:6], AF.Exp, scale=-0.5)

                @B.vector
                def _(vector):
                    vector = kb.wrap(vector, "V")
                    kb.wait(vector, "S", s_rms)
                    vector.tensor_scalar(zb[:], xnb[:], stat[:, 6:7], None, ALU.mult)
                v_zb = kb.cnt["V"]

                tr_done = []
                for dt in range(8):
                    @B.tensor
                    def _(tensor):
                        kb.wait(tensor, "V", kb.cnt["V"])
                        kb.wait(tensor, "S")
                        kb.inc(tensor.transpose(trp[dt % 2][:], zb[:, dt * 128:(dt + 1) * 128], eye_sb[:]), "Tn")
                    tr_done.append(kb.cnt["Tn"])
                    sba.wait("Tn", tr_done[-1])
                    sba.act(xrT_dst[:, dt, tt * 128:(tt + 1) * 128], trp[dt % 2][:],
                            AF.Copy, scale=normw_sb[:, dt:dt + 1])
                sba.flush(B)
                yield
    xqT_sb = st_xrB.enter_context(nc.sbuf_tensor("xqT_sb", [128, 8, T], BF16))
    drive((gen_p0b(), 3), (gen_p1(x_own, xrT_own, True, xqT_sb), 1))

    if upto == "p1":
        return finish()

    with contextlib.ExitStack() as st:
        sb = lambda n, s, dt=F32: st.enter_context(nc.sbuf_tensor(n, s, dt))
        IH = 1408
        NIT = IH // 128
        wgh = sb("p2_wgh", [128, 8, IH], BF16)
        wuh = sb("p2_wuh", [128, 8, IH], BF16)
        wdh = sb("p2_wdh", [128, NIT, D], BF16)
        hbuf = sb("p2_h", [128, 512])
        hb2 = sb("p2_h2", [128, 512])
        hq = [sb(f"p2_hq{j}", [128, 512], BF16) for j in range(2)]
        hqT = [sb(f"p2_hqT{j}", [128, 4, 128], BF16) for j in range(2)]
        fbuf = sb("p2_f", [128, D])
        f0buf = sb("p2_f0", [128, D])
        rbuf2 = sb("p2_r", [128, D])
        qb2 = quant_bufs(st, nc, "p2q", 512)
        ps_g = [st.enter_context(nc.psum_tensor(f"p2_psg{j}", [128, 512], F32)) for j in range(2)]
        ps_u = [st.enter_context(nc.psum_tensor(f"p2_psu{j}", [128, 512], F32)) for j in range(2)]
        ps_d0 = st.enter_context(nc.psum_tensor("p2_psd0", [128, 512], F32))
        ps_d1 = st.enter_context(nc.psum_tensor("p2_psd1", [128, 512], F32))
        ps_t = [st.enter_context(nc.psum_tensor(f"p2_pst{j}", [128, 128], BF16)) for j in range(2)]

        segs = [(0, 512), (512, 512), (1024, 384)]
        NSEG = len(segs)
        for ih in range(2):
            i0 = ih * IH
            @B.sync
            def _(sync):
                kb.wait(sync, "Tn")
                kb.wait(sync, "St")
                kb.dma(sync, wgh[:], swgT.rearrange("(a p) i -> p a i", p=128)[:, :, i0:i0 + IH])
                kb.dma(sync, wuh[:], swuT.rearrange("(a p) i -> p a i", p=128)[:, :, i0:i0 + IH])
                kb.dma(sync, wdh[:], swdT.rearrange("(a p) d -> p a d", p=128)[:, NIT * ih:NIT * (ih + 1), :])
            ldw = kb.cnt["L"]

            for tt in range(min(TT, LIM_TT)):
                sba = SBatch(kb)
                t_gu = [0] * NSEG
                gu_free = [kb.cnt["S"], kb.cnt["S"]]
                gu_free_v = [kb.cnt["V"], kb.cnt["V"]]
                hq_free = [kb.cnt["Tn"], kb.cnt["Tn"]]
                hqT_free = [kb.cnt["Tn"], kb.cnt["Tn"]]
                ev_done = [0] * NSEG
                for si in range(NSEG + 1):
                    if si < NSEG:
                        s0, sl = segs[si]
                        jsi = si % 2

                        @B.tensor
                        def _(tensor):
                            kb.wait(tensor, "L", ldw)
                            kb.wait(tensor, "S", gu_free[jsi])
                            kb.wait(tensor, "V", gu_free_v[jsi])
                            for kd in range(8):
                                tensor.matmul(ps_g[jsi][:, :sl],
                                              xqT_sb[:, kd, tt * 128:(tt + 1) * 128],
                                              wgh[:, kd, s0:s0 + sl],
                                              start=(kd == 0), stop=(kd == 7))
                            for kd in range(8):
                                mi = tensor.matmul(ps_u[jsi][:, :sl],
                                                   xqT_sb[:, kd, tt * 128:(tt + 1) * 128],
                                                   wuh[:, kd, s0:s0 + sl],
                                                   start=(kd == 0), stop=(kd == 7))
                            kb.inc(mi, "Tn")
                        t_gu[si] = kb.cnt["Tn"]

                    if si >= 1:
                        sp = si - 1
                        s0, sl = segs[sp]
                        jsp = sp % 2

                        sba.wait("Tn", t_gu[sp])
                        sba.wait("V", gu_free_v[1 - jsp])
                        s_h = sba.act(hbuf[:, :sl], ps_g[jsp][:, :sl], AF.Silu)
                        gu_free[jsp] = s_h

                        @B.vector
                        def _(vector):
                            vector = kb.wrap(vector, "V")
                            kb.wait(vector, "S", s_h)
                            kb.wait(vector, "Tn", hq_free[jsp])
                            vector.tensor_tensor(hb2[:, :sl], hbuf[:, :sl], ps_u[jsp][:, :sl], ALU.mult)
                            emit_quant(kb, vector, hb2[:, :sl], hq[jsp][:, :sl], qb2, sl)
                        gu_free_v[jsp] = kb.cnt["V"]
                        v_hq = kb.cnt["V"]

                        nsub = sl // 128
                        evs = 0
                        for isub in range(nsub):
                            pj = isub % 2
                            @B.tensor
                            def _(tensor):
                                kb.wait(tensor, "V", v_hq)
                                kb.wait(tensor, "S")
                                kb.inc(tensor.transpose(ps_t[pj][:], hq[jsp][:, isub * 128:(isub + 1) * 128], eye_sb[:]), "Tn")
                            tj = kb.cnt["Tn"]

                            sba.wait("Tn", max(tj, hqT_free[jsp]))
                            evs = sba.act(hqT[jsp][:, isub, :], ps_t[pj][:], AF.Copy)
                        hq_free[jsp] = kb.cnt["Tn"]

                        @B.tensor
                        def _(tensor):
                            kb.wait(tensor, "S", evs)
                            it_base = s0 // 128
                            for isub in range(nsub):
                                it = it_base + isub
                                first = (sp == 0 and isub == 0)
                                last = (sp == NSEG - 1 and isub == nsub - 1)
                                tensor.matmul(ps_d0[:], hqT[jsp][:, isub, :], wdh[:, it, 0:512],
                                              start=first, stop=last)
                                mi = tensor.matmul(ps_d1[:], hqT[jsp][:, isub, :], wdh[:, it, 512:1024],
                                                   start=first, stop=last)
                            kb.inc(mi, "Tn")
                        hqT_free[jsp] = kb.cnt["Tn"]
                t_d = kb.cnt["Tn"]

                if ih == 0:
                    sba.wait("Tn", t_d)
                    sba.wait("St")
                    sba.act(fbuf[:, 0:512], ps_d0[:], AF.Copy)
                    s_f = sba.act(fbuf[:, 512:1024], ps_d1[:], AF.Copy)
                    sba.flush(B)

                    @B.sync
                    def _(sync):
                        kb.wait(sync, "S", s_f)
                        kb.dma(sync, sffn[tt * 128:(tt + 1) * 128, :], fbuf[:], sem="St")
                else:
                    @B.sync
                    def _(sync):
                        kb.wait(sync, "V")
                        kb.wait(sync, "St")
                        kb.dma(sync, f0buf[:], sffn[tt * 128:(tt + 1) * 128, :])
                        kb.dma(sync, rbuf2[:], racc[tt * 128:(tt + 1) * 128, :])
                    ldr = kb.cnt["L"]

                    sba.wait("Tn", t_d)
                    sba.wait("St")
                    sba.act(fbuf[:, 0:512], ps_d0[:], AF.Copy)
                    s_f = sba.act(fbuf[:, 512:1024], ps_d1[:], AF.Copy)
                    sba.flush(B)

                    @B.vector
                    def _(vector):
                        vector = kb.wrap(vector, "V")
                        kb.wait(vector, "S", s_f)
                        kb.wait(vector, "L", ldr)
                        vector.tensor_tensor(fbuf[:], fbuf[:], f0buf[:], ALU.add)
                        vector.tensor_tensor(fbuf[:], fbuf[:], rbuf2[:], ALU.add)
                    v_f = kb.cnt["V"]

                    @B.sync
                    def _(sync):
                        kb.wait(sync, "V", v_f)
                        kb.dma(sync, racc[tt * 128:(tt + 1) * 128, :], fbuf[:], sem="St")

    st_xrB.close()
    st_xrB = contextlib.ExitStack()
    xrT_pre = st_xrB.enter_context(nc.sbuf_tensor("xrT_pre", [128, 8, T], BF16))
    for _ in gen_p1(x_pre, xrT_pre, False, None):
        pass

    def p3_chunks(xrT_src, own):
        with contextlib.ExitStack() as st:
            sb = lambda n, s, dt=F32: st.enter_context(nc.sbuf_tensor(n, s, dt))
            pre = "c3o" if own else "c3p"
            inw_sb = sb(pre + "_inw", [128, 8, E], BF16)
            tmpc = sb(pre + "_tmpc", [128, E])
            xib = [sb(pre + f"_xi{j}", [128, 3 + CH]) for j in range(2)]
            cb = sb(pre + "_cb", [128, CH])
            cb2 = sb(pre + "_cb2", [128, CH])
            cb3 = sb(pre + "_cb3", [128, CH])
            xcb = sb(pre + "_xc", [128, CH])
            xcbf = [sb(pre + f"_xcbf{j}", [128, CH], BF16) for j in range(2)]
            dlowb = sb(pre + "_dlow", [DTR, CH], BF16)
            zb = sb(pre + "_zb", [128, CH])
            db = sb(pre + "_db", [128, CH])
            bcb = sb(pre + "_bcb", [NS, 2 * CH], BF16)
            if own:
                skw_sb = sb(pre + "_skw", [128, 8, E], BF16)
                skb = sb(pre + "_skb", [128, CH])
            ps_x = [st.enter_context(nc.psum_tensor(pre + f"_psx{j}", [128, CH], F32)) for j in range(2)]
            ps_dl = st.enter_context(nc.psum_tensor(pre + "_psdl", [DTR, CH], F32))
            ps_B = st.enter_context(nc.psum_tensor(pre + "_psB", [NS, CH], F32))
            ps_C = st.enter_context(nc.psum_tensor(pre + "_psC", [NS, CH], F32))
            ps_d2 = st.enter_context(nc.psum_tensor(pre + "_psd2", [128, CH], F32))
            if own:
                ps_sk = st.enter_context(nc.psum_tensor(pre + "_pssk", [128, CH], F32))

            for dt_ in range(8):
                @B.sync
                def _(sync):
                    kb.wait(sync, "V")
                    kb.wait(sync, "S")
                    kb.wait(sync, "St")
                    kb.wait(sync, "Tn")
                    kb.dma(sync, tmpc[:], inwT.rearrange("(a p) e -> p a e", p=128)[:, dt_, :])
                ldv = kb.cnt["L"]

                @B.vector
                def _(vector):
                    vector = kb.wrap(vector, "V")
                    kb.wait(vector, "L", ldv)
                    kb.inc(vector.tensor_copy(inw_sb[:, dt_, :], tmpc[:]), "V")
            if own:
                for dt_ in range(8):
                    @B.sync
                    def _(sync):
                        kb.wait(sync, "V")
                        kb.dma(sync, tmpc[:], skipwT.rearrange("(a p) e -> p a e", p=128)[:, dt_, :])
                    ldv = kb.cnt["L"]

                    @B.vector
                    def _(vector):
                        vector = kb.wrap(vector, "V")
                        kb.wait(vector, "L", ldv)
                        kb.inc(vector.tensor_copy(skw_sb[:, dt_, :], tmpc[:]), "V")

            for c in range(NCH):
                sba = SBatch(kb)
                c0 = c * CH
                ev_done = [kb.cnt["S"], kb.cnt["S"]]
                cv_done = [kb.cnt["V"], kb.cnt["V"]]
                sg_done = [kb.cnt["Sg"], kb.cnt["Sg"]]
                for et in range(ET):
                    xcf = xcbf[et % 2]
                    jx = et % 2

                    @B.tensor
                    def _(tensor):
                        kb.wait(tensor, "S", ev_done[jx])
                        kb.wait(tensor, "V")
                        for kd in range(8):
                            mi = tensor.matmul(ps_x[jx][:], inw_sb[:, kd, et * 128:(et + 1) * 128],
                                               xrT_src[:, kd, c0:c0 + CH],
                                               start=(kd == 0), stop=(kd == 7))
                        kb.inc(mi, "Tn")
                    t_x = kb.cnt["Tn"]

                    sba.wait("Tn", t_x)
                    sba.wait("V", cv_done[jx])
                    sba.wait("Sg", sg_done[jx])
                    s_xi = sba.act(xib[jx][:, 3:3 + CH], ps_x[jx][:], AF.Copy)
                    ev_done[jx] = s_xi

                    @B.vector
                    def _(vector):
                        vector = kb.wrap(vector, "V")
                        kb.wait(vector, "S", s_xi)
                        vector.tensor_copy(xib[jx][:, 0:3], halo_sb[:, et * 3:(et + 1) * 3])
                        vector.tensor_copy(halo_sb[:, et * 3:(et + 1) * 3], xib[jx][:, CH:CH + 3])
                    v_halo = kb.cnt["V"]

                    @B.vector
                    def _(vector):
                        vector = kb.wrap(vector, "V")
                        kb.wait(vector, "S", kb.cnt["S"])
                        vector.tensor_scalar(cb[:], xib[jx][:, 0:CH], convw_sb[:, et * 4:et * 4 + 1],
                                             None, ALU.mult)
                        vector.scalar_tensor_tensor(cb2[:], xib[jx][:, 1:1 + CH],
                                                    convw_sb[:, et * 4 + 1:et * 4 + 2], cb[:],
                                                    ALU.mult, ALU.add)
                        vector.scalar_tensor_tensor(cb[:], xib[jx][:, 2:2 + CH],
                                                    convw_sb[:, et * 4 + 2:et * 4 + 3], cb2[:],
                                                    ALU.mult, ALU.add)
                        vector.scalar_tensor_tensor(cb2[:], xib[jx][:, 3:3 + CH],
                                                    convw_sb[:, et * 4 + 3:et * 4 + 4], cb[:],
                                                    ALU.mult, ALU.add)
                    v_cv = kb.cnt["V"]
                    cv_done[jx] = v_cv
                    sg_done[jx] = kb.cnt["Sg"]

                    sba.wait("V", v_cv)
                    sba.wait("Stg")
                    s_xc = sba.act(xcb[:], cb2[:], AF.Silu, bias=convb_sb[:, et:et + 1])

                    @B.vector
                    def _(vector):
                        vector = kb.wrap(vector, "V")
                        kb.wait(vector, "S", s_xc)
                        kb.inc(vector.tensor_copy(xcf[:], xcb[:]), "V")
                    v_xcf = kb.cnt["V"]

                    @B.gpsimd
                    def _(gpsimd):
                        kb.wait(gpsimd, "S", s_xc)
                        kb.dma(gpsimd, sxc[et * 128:(et + 1) * 128, c0:c0 + CH], xcb[:], sem="Stg")

                    @B.tensor
                    def _(tensor):
                        kb.wait(tensor, "V", v_xcf)
                        tensor.matmul(ps_dl[:], wd1T_sb[:, et * DTR:(et + 1) * DTR], xcf[:],
                                      start=(et == 0), stop=(et == ET - 1))
                        tensor.matmul(ps_B[:], wbT_sb[:, et * NS:(et + 1) * NS], xcf[:],
                                      start=(et == 0), stop=(et == ET - 1))
                        mi = tensor.matmul(ps_C[:], wcT_sb[:, et * NS:(et + 1) * NS], xcf[:],
                                           start=(et == 0), stop=(et == ET - 1))
                        if own:
                            for kd in range(8):
                                mi = tensor.matmul(ps_sk[:], skw_sb[:, kd, et * 128:(et + 1) * 128],
                                                   xrT_src[:, kd, c0:c0 + CH],
                                                   start=(kd == 0), stop=(kd == 7))
                        kb.inc(mi, "Tn")
                    t_bc = kb.cnt["Tn"]

                    if own:
                        sba.wait("Tn", t_bc)
                        sba.wait("Stg")
                        s_sk = sba.act(skb[:], ps_sk[:], AF.Silu)

                        @B.gpsimd
                        def _(gpsimd):
                            kb.wait(gpsimd, "S", s_sk)
                            kb.dma(gpsimd, sskip[et * 128:(et + 1) * 128, c0:c0 + CH], skb[:], sem="Stg")

                t_dl = kb.cnt["Tn"]
                sba.flush(B)
                sba = SBatch(kb)

                sba.wait("Tn", t_dl)
                sba.wait("Stg")
                sba.act(dlowb[:], ps_dl[:], AF.Copy)
                sba.act(bcb[:, 0:CH], ps_B[:], AF.Copy)
                s_dl = sba.act(bcb[:, CH:2 * CH], ps_C[:], AF.Copy)

                @B.gpsimd
                def _(gpsimd):
                    kb.wait(gpsimd, "S", s_dl)
                    kb.dma(gpsimd, sB[:, c0:c0 + CH], bcb[:, 0:CH], sem="Stg")
                    kb.dma(gpsimd, sC[:, c0:c0 + CH], bcb[:, CH:2 * CH], sem="Stg")

                for et in range(ET):
                    @B.tensor
                    def _(tensor):
                        kb.wait(tensor, "S", s_dl)
                        kb.wait(tensor, "V")
                        kb.inc(tensor.matmul(ps_d2[:], wd2T_sb[:, et * 128:(et + 1) * 128],
                                             dlowb[:], start=True, stop=True), "Tn")
                    t_d2 = kb.cnt["Tn"]

                    @B.vector
                    def _(vector):
                        vector = kb.wrap(vector, "V")
                        kb.wait(vector, "Tn", t_d2)
                        kb.wait(vector, "S")
                        kb.inc(vector.tensor_scalar(zb[:], ps_d2[:], wd2b_sb[:, et:et + 1],
                                                    80.0, ALU.add, ALU.min), "V")
                    v_zb2 = kb.cnt["V"]

                    sba.wait("V", v_zb2)
                    sba.wait("Stg")
                    sba.act(db[:], zb[:], AF.Exp)
                    s_db = sba.act(db[:], db[:], AF.Ln, bias=1.0)

                    @B.gpsimd
                    def _(gpsimd):
                        kb.wait(gpsimd, "S", s_db)
                        kb.dma(gpsimd, sdelta[et * 128:(et + 1) * 128, c0:c0 + CH], db[:], sem="Stg")
                sba.flush(B)

    if upto == "p1pre":
        return finish()

    p3_chunks(xrT_pre, False)
    st_xrB.close()
    if upto == "p3pre":
        return finish()

    def p3_scan(own):
        with contextlib.ExitStack() as st:
            sb = lambda n, s, dt=F32: st.enter_context(nc.sbuf_tensor(n, s, dt))
            pre = "s3o" if own else "s3p"
            Brep = sb(pre + "_Brep", [128, 8, T], BF16)
            dtile = [sb(pre + f"_dt{j}", [128, T]) for j in range(2)]
            xct = [sb(pre + f"_xct{j}", [128, T]) for j in range(2)]
            dxt = [sb(pre + f"_dxt{j}", [128, T]) for j in range(2)]
            ab = [sb(pre + f"_ab{j}", [128, T]) for j in range(2)]
            ub = [sb(pre + f"_ub{j}", [128, T]) for j in range(2)]
            if own:
                Crep = sb(pre + "_Crep", [128, 8, T], BF16)
                hb = [sb(pre + f"_hb{j}", [128, T], BF16) for j in range(2)]
                hcb = [sb(pre + f"_hcb{j}", [128, T], BF16) for j in range(2)]
                ypb = sb(pre + "_ypb", [128, T])
                ytl = sb(pre + "_ytl", [128, T])
                skt = sb(pre + "_skt", [128, T])
                ymb = sb(pre + "_ymb", [128, T], BF16)
                ps_y = [st.enter_context(nc.psum_tensor(pre + f"_psy{j}", [128, 512], F32))
                        for j in range(4)]
                ps_ht = st.enter_context(nc.psum_tensor(pre + "_psht", [128, 128], F32))
            else:
                hb = [sb(pre + f"_hb{j}", [128, T]) for j in range(2)]

            ab_free = [kb.cnt["V"], kb.cnt["V"]]
            ub_free = [kb.cnt["V"], kb.cnt["V"]]
            hc_free = [kb.cnt["Tn"], kb.cnt["Tn"]]
            c0s = (kb.cnt["V"], kb.cnt["S"], kb.cnt["Sg"])
            et_cons = [c0s, c0s]
            ld_t = [0, 0]
            TH = T // 2
            for nh in range(2):
                @B.sync
                def _(sync):
                    kb.wait(sync, "V")
                    kb.wait(sync, "S")
                    kb.wait(sync, "St")
                    kb.wait(sync, "Stg")
                    kb.wait(sync, "Tn")
                    kb.wait(sync, "Sg")
                    kb.dma(sync, Brep[:], sB[nh * 8:(nh + 1) * 8, :].partition_broadcast(128))
                    if own:
                        kb.dma(sync, Crep[:], sC[nh * 8:(nh + 1) * 8, :].partition_broadcast(128))
                    kb.dma(sync, dtile[0][:], sdelta[0:128, :])
                    kb.dma(sync, xct[0][:], sxc[0:128, :])
                ld_t[0] = kb.cnt["L"]

                for et in range(ET):
                    j_et = et % 2
                    ldt = ld_t[j_et]

                    @B.vector
                    def _(vector):
                        vector = kb.wrap(vector, "V")
                        kb.wait(vector, "L", ldt)
                        vector.tensor_tensor(dxt[j_et][:], dtile[j_et][:], xct[j_et][:], ALU.mult)
                    v_dx = kb.cnt["V"]

                    if et + 1 < ET:
                        jn = (et + 1) % 2
                        @B.sync
                        def _(sync):
                            cv, cs, cg = et_cons[jn]
                            kb.wait(sync, "V", cv)
                            kb.wait(sync, "S", cs)
                            kb.wait(sync, "Sg", cg)
                            kb.wait(sync, "L", ld_t[j_et])
                            kb.dma(sync, dtile[jn][:], sdelta[(et + 1) * 128:(et + 2) * 128, :])
                            kb.dma(sync, xct[jn][:], sxc[(et + 1) * 128:(et + 2) * 128, :])
                        ld_t[jn] = kb.cnt["L"]

                    if own and nh == 1:
                        @B.sync
                        def _(sync):
                            kb.wait(sync, "V")
                            kb.wait(sync, "L", ld_t[(et + 1) % 2 if et + 1 < ET else j_et])
                            kb.dma(sync, skt[:], sskip[et * 128:(et + 1) * 128, :])
                            kb.dma(sync, ypb[:], sypart[et * 128:(et + 1) * 128, :])
                        ld_sk = kb.cnt["L"]
                    else:
                        ld_sk = 0

                    sba = SBatch(kb)
                    s_ab = [0] * 8
                    g_u = [0] * 8
                    for n in range(8):
                        j = n % 2
                        na = nh * 8 + n
                        sba.wait("L", ldt)
                        if ab_free[j]:
                            sba.wait("V", ab_free[j])
                        s_ab[n] = sba.act(ab[j][:], dtile[j_et][:], AF.Exp,
                                          scale=A_sb[:, et * NS + na:et * NS + na + 1])

                        @B.vector
                        def _(vector):
                            vector = kb.wrap(vector, "V")
                            if ub_free[j]:
                                kb.wait(vector, "V", ub_free[j])
                            vector.tensor_tensor(ub[j][:], dxt[j_et][:],
                                                 Brep[:, n, :], ALU.mult)
                            kb.wait(vector, "S", s_ab[n])
                            if own and hc_free[j]:
                                kb.wait(vector, "Tn", hc_free[j])
                            init = h0_sb[:, et * NS + na:et * NS + na + 1] if own else zero1[:]
                            vector.tensor_tensor_scan(hb[j][:], ab[j][:], ub[j][:],
                                                      init, ALU.mult, ALU.add)
                            ab_free[j] = ub_free[j] = kb.cnt["V"]
                            if own:
                                vector.tensor_tensor(hcb[j][:], hb[j][:], Crep[:, n, :],
                                                     ALU.mult)
                            else:
                                vector.tensor_copy(h0_sb[:, et * NS + na:et * NS + na + 1],
                                                   hb[j][:, T - 1:T])
                        v_hc = kb.cnt["V"]

                        if own:
                            @B.tensor
                            def _(tensor):
                                for pc in (v_hc - 2, v_hc - 1, v_hc):
                                    kb.wait(tensor, "V", pc)
                                    tensor.matmul(ps_ht[:], eye_sb[:], eye_sb[:, 0:128],
                                                  start=True, stop=True)
                                kb.wait(tensor, "V", v_hc)
                                if n == 0:
                                    kb.wait(tensor, "S")
                                    kb.wait(tensor, "V")
                                for sg in range(4):
                                    mi = tensor.matmul(ps_y[sg][:], eye_sb[:],
                                                       hcb[j][:, sg * 512:(sg + 1) * 512],
                                                       start=(n == 0), stop=(n == 7))
                                kb.inc(mi, "Tn")
                                hc_free[j] = kb.cnt["Tn"]
                    t_y = kb.cnt["Tn"]

                    if own:
                        if nh == 0:
                            sba.wait("Tn", t_y)
                            sba.wait("Stg")
                            for sg in range(4):
                                s_y = sba.act(ytl[:, sg * 512:(sg + 1) * 512],
                                              ps_y[sg][:], AF.Copy)
                            sba.flush(B)

                            @B.gpsimd
                            def _(gpsimd):
                                kb.wait(gpsimd, "S", s_y)
                                kb.dma(gpsimd, sypart[et * 128:(et + 1) * 128, :], ytl[:], sem="Stg")
                        else:
                            sba.flush(B)

                            @B.vector
                            def _(vector):
                                vector = kb.wrap(vector, "V")
                                kb.wait(vector, "Tn", t_y)
                                kb.wait(vector, "L", ld_sk)
                                kb.wait(vector, "Stg")
                                for sg in range(4):
                                    vector.scalar_tensor_tensor(
                                        ytl[:, sg * 512:(sg + 1) * 512],
                                        xct[j_et][:, sg * 512:(sg + 1) * 512],
                                        Wd_sb[:, et:et + 1],
                                        ps_y[sg][:], ALU.mult, ALU.add)
                                vector.tensor_tensor(ytl[:], ytl[:], ypb[:], ALU.add)
                                vector.tensor_tensor(ymb[:], ytl[:], skt[:], ALU.mult)
                            v_ym = kb.cnt["V"]

                            @B.gpsimd
                            def _(gpsimd):
                                kb.wait(gpsimd, "V", v_ym)
                                kb.dma(gpsimd, sym[et * 128:(et + 1) * 128, :], ymb[:], sem="Stg")
                    sba.flush(B)
                    et_cons[j_et] = (kb.cnt["V"], kb.cnt["S"], kb.cnt["Sg"])

            if not own:
                @B.vector
                def _(vector):
                    vector = kb.wrap(vector, "V")
                    vector.tensor_scalar(h0_sb[:], h0_sb[:], maskv[:, 0:1], None,
                                         ALU.mult)
                    vector.tensor_scalar(halo_sb[:], halo_sb[:], maskv[:, 0:1], None,
                                         ALU.mult)


    def p3_prefix():
        with contextlib.ExitStack() as st:
            sb = lambda n, s, dt=F32: st.enter_context(nc.sbuf_tensor(n, s, dt))
            Brep = sb("pf_Brep", [128, NS, T], BF16)
            dtile = [sb(f"pf_dt{j}", [128, T]) for j in range(2)]
            xct = [sb(f"pf_xct{j}", [128, T]) for j in range(2)]
            dxt = [sb(f"pf_dxt{j}", [128, T]) for j in range(2)]
            Eb = [sb(f"pf_E{j}", [128, T]) for j in range(2)]
            ub = [sb(f"pf_ub{j}", [128, T]) for j in range(2)]
            cbuf = sb("pf_c", [128, T])
            onescol = sb("pf_ones", [128, 1])
            bias2 = sb("pf_bias", [128, NS])
            acc1 = sb("pf_acc", [128, 1])
            ps_hp = st.enter_context(nc.psum_tensor("pf_psht", [128, 128], F32))
            ones = onescol[:].broadcast_to((128, T))

            @B.vector
            def _(vector):
                vector = kb.wrap(vector, "V")
                vector.memset(onescol[:], 1.0)

            E_free = [kb.cnt["V"], kb.cnt["V"]]
            u_free = [kb.cnt["V"], kb.cnt["V"]]
            et_cons = [(kb.cnt["V"], kb.cnt["S"], kb.cnt["Sg"])] * 2
            ld_t = [0, 0]
            TH = T // 2

            @B.sync
            def _(sync):
                kb.wait(sync, "V")
                kb.wait(sync, "S")
                kb.wait(sync, "St")
                kb.wait(sync, "Stg")
                kb.wait(sync, "Tn")
                kb.wait(sync, "Sg")
                kb.dma(sync, Brep[:], sB.partition_broadcast(128))
                kb.dma(sync, dtile[0][:], sdelta[0:128, :])
                kb.dma(sync, xct[0][:], sxc[0:128, :])
            ld_t[0] = kb.cnt["L"]

            for et in range(ET):
                j_et = et % 2
                ldt = ld_t[j_et]

                @B.vector
                def _(vector):
                    vector = kb.wrap(vector, "V")
                    kb.wait(vector, "L", ldt)
                    vector.tensor_tensor(dxt[j_et][:], dtile[j_et][:], xct[j_et][:], ALU.mult)
                    vector.tensor_tensor_scan(cbuf[:], ones, dtile[j_et][:],
                                              zero1[:], ALU.mult, ALU.add)
                v_dx = kb.cnt["V"]

                if et + 1 < ET:
                    jn = (et + 1) % 2
                    @B.sync
                    def _(sync):
                        cv, cs, cg = et_cons[jn]
                        kb.wait(sync, "V", cv)
                        kb.wait(sync, "Sg", cg)
                        kb.dma(sync, dtile[jn][:], sdelta[(et + 1) * 128:(et + 2) * 128, :])
                        kb.dma(sync, xct[jn][:], sxc[(et + 1) * 128:(et + 2) * 128, :])
                    ld_t[jn] = kb.cnt["L"]

                sba = SBatch(kb)
                s_E = [0] * NS
                g_u = [0] * NS
                v_bias = [0] * NS
                for n in range(NS):
                    j = n % 2
                    col = et * NS + n

                    @B.vector
                    def _(vector):
                        vector = kb.wrap(vector, "V")
                        vector.tensor_tensor(bias2[:, n:n + 1], A_sb[:, col:col + 1],
                                             cbuf[:, T - 1:T], ALU.mult)
                    v_bias[n] = kb.cnt["V"]

                for n in range(NS):
                    j = n % 2
                    col = et * NS + n
                    sba.wait("V", v_bias[n])
                    if E_free[j]:
                        sba.wait("V", E_free[j])
                    s_E[n] = sba.act(Eb[j][:], cbuf[:], AF.Exp,
                                     scale=nA_sb[:, col:col + 1],
                                     bias=bias2[:, n:n + 1])

                    @B.gpsimd
                    def _(gpsimd):
                        gpsimd = kb.wrap(gpsimd, "Sg")
                        kb.wait(gpsimd, "V", v_dx)
                        if u_free[j]:
                            kb.wait(gpsimd, "V", u_free[j])
                        gpsimd.tensor_tensor(ub[j][:, 0:1536], dxt[j_et][:, 0:1536],
                                             Brep[:, n, 0:1536], ALU.mult)
                    g_u[n] = kb.cnt["Sg"]

                    pass

                    @B.vector
                    def _(vector):
                        vector = kb.wrap(vector, "V")
                        if u_free[j]:
                            kb.wait(vector, "V", u_free[j])
                        vector.tensor_tensor(ub[j][:, 1536:T], dxt[j_et][:, 1536:T],
                                             Brep[:, n, 1536:T], ALU.mult)
                        kb.wait(vector, "S", s_E[n])
                        kb.wait(vector, "Sg", g_u[n])
                        vector.scalar_tensor_tensor(Eb[j][:], Eb[j][:], 1.0, ub[j][:],
                                                    ALU.mult, ALU.mult, accum_out=acc1[:])
                        E_free[j] = u_free[j] = kb.cnt["V"]
                        vector.tensor_copy(h0_sb[:, col:col + 1], acc1[:])
                    v_pf = kb.cnt["V"]

                    @B.tensor
                    def _(tensor):
                        for pc in (v_pf - 2, v_pf - 1, v_pf):
                            kb.wait(tensor, "V", pc)
                            tensor.matmul(ps_hp[:], eye_sb[:], eye_sb[:, 0:128],
                                          start=True, stop=True)
                        kb.inc(tensor.nop() if False else tensor.matmul(
                            ps_hp[:], eye_sb[:], eye_sb[:, 0:128], start=True, stop=True), "Tn")
                sba.flush(B)

                et_cons[j_et] = (kb.cnt["V"], kb.cnt["S"], kb.cnt["Sg"])

            @B.vector
            def _(vector):
                vector = kb.wrap(vector, "V")
                vector.tensor_scalar(h0_sb[:], h0_sb[:], maskv[:, 0:1], None, ALU.mult)
                vector.tensor_scalar(halo_sb[:], halo_sb[:], maskv[:, 0:1], None, ALU.mult)

    p3_prefix()
    if upto == "p3scanpre":
        return finish()

    p3_chunks(xrT_own, True)
    st_xrA.close()
    if upto == "p3own":
        return finish()
    p3_scan(True)
    if upto == "p3scanown":
        return finish()

    with contextlib.ExitStack() as st:
        sb = lambda n, s, dt=F32: st.enter_context(nc.sbuf_tensor(n, s, dt))
        oww = sb("c4_oww", [128, ET, D], BF16)
        tmpc = sb("c4_tmp", [128, D])
        ymt = sb("c4_ymt", [128, ET, 128], BF16)
        rtile = sb("c4_r", [128, D])
        ob = sb("c4_o", [128, D])
        ps_o = [st.enter_context(nc.psum_tensor(f"c4_pso{j}", [128, 512], F32)) for j in range(2)]

        for et in range(ET):
            @B.sync
            def _(sync):
                kb.wait(sync, "V")
                kb.wait(sync, "S")
                kb.wait(sync, "St")
                kb.wait(sync, "Tn")
                kb.dma(sync, tmpc[:], outwT.rearrange("(a p) d -> p a d", p=128)[:, et, :])
            ldv = kb.cnt["L"]

            @B.vector
            def _(vector):
                vector = kb.wrap(vector, "V")
                kb.wait(vector, "L", ldv)
                kb.inc(vector.tensor_copy(oww[:, et, :], tmpc[:]), "V")

        for tt in range(TT):
            @B.sync
            def _(sync):
                kb.wait(sync, "V")
                kb.wait(sync, "St")
                kb.wait(sync, "Stg")
                kb.dma(sync, ymt[:], sym.rearrange("(a p) t -> p a t", p=128)[:, :, tt * 128:(tt + 1) * 128])
                kb.dma(sync, rtile[:], racc[tt * 128:(tt + 1) * 128, :])
            ldy = kb.cnt["L"]

            @B.tensor
            def _(tensor):
                kb.wait(tensor, "L", ldy)
                kb.wait(tensor, "V")
                kb.wait(tensor, "S")
                for ds in range(2):
                    for et in range(ET):
                        mi = tensor.matmul(ps_o[ds][:], ymt[:, et, :],
                                           oww[:, et, ds * 512:(ds + 1) * 512],
                                           start=(et == 0), stop=(et == ET - 1))
                kb.inc(mi, "Tn")
            t_o = kb.cnt["Tn"]

            @B.vector
            def _(vector):
                vector = kb.wrap(vector, "V")
                kb.wait(vector, "Tn", t_o)
                kb.wait(vector, "St")
                for ds in range(2):
                    vi = vector.tensor_tensor(ob[:, ds * 512:(ds + 1) * 512], ps_o[ds][:],
                                              rtile[:, ds * 512:(ds + 1) * 512], ALU.add)
                kb.inc(vi, "V")
            v_o = kb.cnt["V"]

            @B.sync
            def _(sync):
                kb.wait(sync, "V", v_o)
                kb.dma(sync, out[tt * 128:(tt + 1) * 128, :], ob[:], sem="St")

    @B.sync
    def _(sync):
        kb.wait(sync, "St")
        kb.wait(sync, "Stg")
        kb.wait(sync, "Lg")
        if dbg is not None:
            kb.dma(sync, dbg_out, dbgmap[dbg][0], sem="St")
            kb.wait(sync, "St")

    B.__exit__(None, None, None)
    return nc, kb


def host_inputs(resid, weights, core):
    b, h = core // 2, core % 2
    w = weights
    m = {
        "x_own": np.ascontiguousarray(resid[b, h * T:(h + 1) * T]),
        "x_pre": np.ascontiguousarray(resid[b, :T]),
        "maskin": np.array([[float(h)]], np.float32),
        "normw_c": np.ascontiguousarray(w["norm_w"].reshape(8, 128).T),
        "inwT": np.ascontiguousarray(w["in_w"].T),
        "skipwT": np.ascontiguousarray(w["skip_w"].T),
        "outwT": np.ascontiguousarray(w["out_w"].T),
        "wd1T_c": np.ascontiguousarray(w["wd1"].T.reshape(ET, 128, DTR).transpose(1, 0, 2).reshape(128, ET * DTR)),
        "wd2T": np.ascontiguousarray(w["wd2"].T),
        "wbT_c": np.ascontiguousarray(w["wb"].T.reshape(ET, 128, NS).transpose(1, 0, 2).reshape(128, ET * NS)),
        "wcT_c": np.ascontiguousarray(w["wc"].T.reshape(ET, 128, NS).transpose(1, 0, 2).reshape(128, ET * NS)),
        "wd2b_c": np.ascontiguousarray(w["wd2_b"].reshape(ET, 128).T),
        "convb_c": np.ascontiguousarray(w["conv_b"].reshape(ET, 128).T),
        "convw_c": np.ascontiguousarray(w["conv_w"].reshape(ET, 128, 4).transpose(1, 0, 2).reshape(128, ET * 4)),
        "Alog_c": np.ascontiguousarray(w["A_log"].reshape(ET, 128, NS).transpose(1, 0, 2).reshape(128, ET * NS)),
        "Wd_c": np.ascontiguousarray(w["W_D"].reshape(ET, 128).T),
        "wg": np.ascontiguousarray(w["w_gate"]),
        "wu": np.ascontiguousarray(w["w_up"]),
        "wdn": np.ascontiguousarray(w["w_down"]),
        "eye_in": np.eye(128, dtype=np.float32).astype(ml_dtypes.bfloat16),
    }
    return m


def kernel(L0_attn_resid, L0_attn_norm_w, L0_attn_skip_w, L0_attn_in_w,
           L0_attn_conv_w, L0_attn_conv_b, L0_attn_wd1, L0_attn_wd2,
           L0_attn_wd2_b, L0_attn_wb, L0_attn_wc, L0_attn_A_log, L0_attn_W_D,
           L0_attn_out_w, L0_ffn_w_gate, L0_ffn_w_up, L0_ffn_w_down):
    from concourse.bass_utils import run_bass_kernel_spmd
    resid = np.ascontiguousarray(np.asarray(L0_attn_resid, dtype=np.float32))
    w = {
        "norm_w": np.asarray(L0_attn_norm_w, np.float32),
        "skip_w": np.asarray(L0_attn_skip_w, np.float32),
        "in_w": np.asarray(L0_attn_in_w, np.float32),
        "conv_w": np.asarray(L0_attn_conv_w, np.float32),
        "conv_b": np.asarray(L0_attn_conv_b, np.float32),
        "wd1": np.asarray(L0_attn_wd1, np.float32),
        "wd2": np.asarray(L0_attn_wd2, np.float32),
        "wd2_b": np.asarray(L0_attn_wd2_b, np.float32),
        "wb": np.asarray(L0_attn_wb, np.float32),
        "wc": np.asarray(L0_attn_wc, np.float32),
        "A_log": np.asarray(L0_attn_A_log, np.float32),
        "W_D": np.asarray(L0_attn_W_D, np.float32),
        "out_w": np.asarray(L0_attn_out_w, np.float32),
        "w_gate": np.asarray(L0_ffn_w_gate, np.float32),
        "w_up": np.asarray(L0_ffn_w_up, np.float32),
        "w_down": np.asarray(L0_ffn_w_down, np.float32),
    }
    nc, kb = build_nc()
    in_maps = [host_inputs(resid, w, c) for c in range(8)]
    res = run_bass_kernel_spmd(nc, in_maps, core_ids=list(range(8)))
    out = np.zeros((4, 4096, 1024), np.float32)
    for c in range(8):
        b, h = c // 2, c % 2
        out[b, h * T:(h + 1) * T] = res.results[c]["out"]
    return out


# version 23 (final)
# speedup vs baseline: 1.0207x; 1.0207x over previous
import contextlib
import numpy as np
import ml_dtypes
import concourse.bass as bass
import concourse.mybir as mybir

F32 = mybir.dt.float32
BF16 = mybir.dt.bfloat16
U8 = mybir.dt.uint8
U32 = mybir.dt.uint32
AF = mybir.ActivationFunctionType
ALU = mybir.AluOpType
AX = mybir.AxisListType

D, E, NS, DTR, I, T = 1024, 2048, 16, 64, 2816, 2048
ET, TT = E // 128, T // 128
RM = float(1.5 * 2 ** 23)
CH = 512
NCH = T // CH
LIM_TT = 16
LIM_W = 100
LIM_CH = 4
LIM_ET = 16
P2_LEVEL = 4


_WRAP_OPS = {
    "tensor_tensor", "tensor_scalar", "tensor_copy", "tensor_reduce",
    "tensor_tensor_scan", "scalar_tensor_tensor", "copy_predicated",
    "tensor_tensor_reduce",
    "reciprocal", "memset", "select", "bn_stats", "bn_aggr", "iota",
    "tensor_add", "tensor_mul", "tensor_sub", "tensor_max", "tensor_relu",
    "tensor_scalar_add", "tensor_scalar_mul", "tensor_scalar_max",
    "tensor_scalar_min", "tensor_scalar_sub", "reduce_sum", "reduce_max",
    "activation", "copy", "add", "mul", "sign", "sqrt", "square",
    "stream_shuffle", "pool", "pool_avg", "pool_max",
}


class EngProxy:

    def __init__(self, kb, eng, sem):
        self._kb, self._eng, self._sem = kb, eng, sem

    def __getattr__(self, name):
        attr = getattr(self._eng, name)
        if name not in _WRAP_OPS:
            return attr
        kb, sem, eng = self._kb, self._sem, self._eng

        def wrapped(*a, **k):
            if kb.cnt[sem] > 0:
                eng.wait_ge(kb.sems[sem], kb.cnt[sem])
            instr = attr(*a, **k)
            instr.then_inc(kb.sems[sem], 1)
            kb.cnt[sem] += 1
            return instr
        return wrapped



class SBatch:

    def __init__(self, kb):
        self.kb = kb
        self.items = []

    def wait(self, sem, val=None):
        v = self.kb.cnt[sem] if val is None else val
        if v > 0:
            self.items.append(("wait", sem, v))

    def act(self, out, in_, func, scale=None, bias=None, accum_out=None):
        kb = self.kb
        kb.cnt["S"] += 1
        my = kb.cnt["S"]
        self.items.append(("act", my, out, in_, func, scale, bias, accum_out))
        return my

    def flush(self, B):
        if not self.items:
            return
        kb, items = self.kb, self.items
        self.items = []

        @B.scalar
        def _(scalar):
            for it in items:
                if it[0] == "wait":
                    _, sem, v = it
                    scalar.wait_ge(kb.sems[sem], v)
                else:
                    _, my, out, in_, func, scale, bias, accum_out = it
                    if my > 1:
                        scalar.wait_ge(kb.sems["S"], my - 1)
                    kw = {}
                    if scale is not None:
                        kw["scale"] = scale
                    if bias is not None:
                        kw["bias"] = bias
                    if accum_out is not None:
                        kw["accum_out"] = accum_out
                    scalar.activation(out=out, in_=in_, func=func, **kw).then_inc(kb.sems["S"], 1)



class KB:
    def __init__(self, nc):
        self.nc = nc
        self.B = None
        self.sems = {}
        self.cnt = {}
        for name in ["L", "L2", "St", "Sg", "V", "S", "Tn", "Lg", "Stg"]:
            self.sems[name] = nc.alloc_semaphore("sem_" + name)
            self.cnt[name] = 0

    def wrap(self, eng, sem):
        return EngProxy(self, eng, sem)

    def inc(self, instr, sem, n=1):
        if sem in ("V", "S", "G"):
            return self.cnt[sem]
        instr.then_inc(self.sems[sem], n)
        self.cnt[sem] += n
        return self.cnt[sem]

    def dma(self, eng, out, in_, sem="L"):
        if self.cnt[sem] > 0:
            eng.wait_ge(self.sems[sem], self.cnt[sem])
        return self.inc(eng.dma_start(out=out, in_=in_), sem, 16)

    def wait(self, eng, sem, val=None):
        v = self.cnt[sem] if val is None else val
        if v > 0:
            eng.wait_ge(self.sems[sem], v)


def emit_quant(kb, vector, x_ap, out_bf_ap, bufs, F):
    NB = F // 16
    bm, z, t1, qb, m8 = bufs["bm"], bufs["z"], bufs["t1"], bufs["qb"], bufs["m8"]
    x3 = x_ap.rearrange("p (m k) -> p m k", k=16)
    vector.tensor_reduce(bm[:, 0:NB], x3, axis=AX.X, op=ALU.max, apply_absolute_value=True)
    vector.tensor_scalar(bm[:, NB:2 * NB], bm[:, 0:NB], 6e-12, None, ALU.max)
    vector.reciprocal(bm[:, 2 * NB:3 * NB], bm[:, NB:2 * NB])
    rb = bm[:, 2 * NB:3 * NB].unsqueeze(-1).broadcast_to((128, NB, 16))
    vector.scalar_tensor_tensor(z[:, :F].rearrange("p (m k) -> p m k", k=16),
                                x3, 6.0, rb, ALU.mult, ALU.mult)
    vector.tensor_scalar(t1[:, :F], z[:, :F], 2.0, RM, ALU.mult, ALU.add)
    vector.tensor_scalar(t1[:, :F], t1[:, :F], -RM, 0.5, ALU.add, ALU.mult)
    vector.tensor_scalar(qb[:, :F].bitcast(U32), z[:, :F].bitcast(U32),
                         0x00200000, None, ALU.add)
    vector.tensor_scalar(qb[:, :F].bitcast(U32), qb[:, :F].bitcast(U32),
                         0xFFC00000, None, ALU.bitwise_and)
    vector.tensor_scalar(z[:, :F].bitcast(U32), z[:, :F].bitcast(U32),
                         0x7FFFFFFF, None, ALU.bitwise_and)
    vector.tensor_scalar(m8[:, :F], z[:, :F], 2.0, None, ALU.is_ge)
    vector.copy_predicated(t1[:, :F], m8[:, :F], qb[:, :F])
    vector.tensor_scalar(bm[:, NB:2 * NB], bm[:, NB:2 * NB], 1.0 / 6.0, None, ALU.mult)
    rb2 = bm[:, NB:2 * NB].unsqueeze(-1).broadcast_to((128, NB, 16))
    return vector.tensor_tensor(out_bf_ap.rearrange("p (m k) -> p m k", k=16),
                                t1[:, :F].rearrange("p (m k) -> p m k", k=16),
                                rb2, ALU.mult)


def quant_bufs(st, nc, pre, F):
    sb = lambda n, s, d=F32: st.enter_context(nc.sbuf_tensor(n, s, d))
    return {
        "bm": sb(pre + "_bm", [128, 3 * (F // 16)]),
        "z": sb(pre + "_z", [128, F]),
        "t1": sb(pre + "_t1", [128, F]),
        "qb": sb(pre + "_qb", [128, F]),
        "m8": sb(pre + "_m8", [128, F], U8),
    }


def build_nc(upto="full", dbg=None):
    nc = bass.Bass("TRN2", target_bir_lowering=False)
    kb = KB(nc)

    dr = lambda n, s, dt=F32, kind="ExternalInput": nc.dram_tensor(n, s, dt, kind=kind).ap()
    x_own = dr("x_own", [T, D])
    x_pre = dr("x_pre", [T, D])
    maskin = dr("maskin", [1, 1])
    normw_c = dr("normw_c", [128, 8])
    inwT = dr("inwT", [D, E])
    skipwT = dr("skipwT", [D, E])
    outwT = dr("outwT", [E, D])
    wd1T_c = dr("wd1T_c", [128, ET * DTR])
    wd2T = dr("wd2T", [DTR, E])
    wbT_c = dr("wbT_c", [128, ET * NS])
    wcT_c = dr("wcT_c", [128, ET * NS])
    wd2b_c = dr("wd2b_c", [128, ET])
    convb_c = dr("convb_c", [128, ET])
    convw_c = dr("convw_c", [128, ET * 4])
    Alog_c = dr("Alog_c", [128, ET * NS])
    Wd_c = dr("Wd_c", [128, ET])
    wg = dr("wg", [I, D])
    wu = dr("wu", [I, D])
    wdn = dr("wdn", [D, I])
    eye_in = dr("eye_in", [128, 128], BF16)
    out = dr("out", [T, D], F32, kind="ExternalOutput")

    di = lambda n, s, dt=F32: nc.dram_tensor(n, s, dt, kind="Internal").ap()
    racc = di("racc", [T, D])
    sdelta = di("sdelta", [E, T])
    sxc = di("sxc", [E, T])
    sskip = di("sskip", [E, T])
    sym = di("sym", [E, T], BF16)
    sypart = di("sypart", [E, T])
    sB = di("sB", [NS, T], BF16)
    sC = di("sC", [NS, T], BF16)
    sffn = di("sffn", [T, D])
    swgT = di("swgT", [D, I], BF16)
    swuT = di("swuT", [D, I], BF16)
    swdT = di("swdT", [I, D], BF16)

    sbp = lambda n, s, dt=F32: nc.alloc_sbuf_tensor(n, s, dt)
    eye_sb = sbp("eye_sb", [128, 128], BF16)
    normw_sb = sbp("normw_sb", [128, 8])
    wd1T_sb = sbp("wd1T_sb", [128, ET * DTR], BF16)
    wd2T_sb = sbp("wd2T_sb", [DTR, E], BF16)
    wbT_sb = sbp("wbT_sb", [128, ET * NS], BF16)
    wcT_sb = sbp("wcT_sb", [128, ET * NS], BF16)
    wd2b_sb = sbp("wd2b_sb", [128, ET])
    convb_sb = sbp("convb_sb", [128, ET])
    convw_sb = sbp("convw_sb", [128, ET * 4])
    A_sb = sbp("A_sb", [128, ET * NS])
    nA_sb = sbp("nA_sb", [128, ET * NS])
    Wd_sb = sbp("Wd_sb", [128, ET])
    maskv = sbp("maskv", [128, 1])
    h0_sb = sbp("h0_sb", [128, ET * NS])
    halo_sb = sbp("halo_sb", [128, ET * 3])
    zero1 = sbp("zero1", [128, 1])
    stat = sbp("stat", [128, 8])

    st_xrA = contextlib.ExitStack()
    st_xrB = contextlib.ExitStack()
    xrT_own = st_xrA.enter_context(nc.sbuf_tensor("xrT_own", [128, 8, T], BF16))

    B = nc.Block()
    B.__enter__()
    kb.B = B

    with nc.sbuf_tensor("p0_tmp", [128, E], F32) as tmpw:
        @B.sync
        def _(sync):
            kb.dma(sync, eye_sb[:], eye_in)
            kb.dma(sync, normw_sb[:], normw_c)
            kb.dma(sync, wd2b_sb[:], wd2b_c)
            kb.dma(sync, convb_sb[:], convb_c)
            kb.dma(sync, convw_sb[:], convw_c)
            kb.dma(sync, A_sb[:], Alog_c)
            kb.dma(sync, Wd_sb[:], Wd_c)
            kb.dma(sync, maskv[:], maskin.broadcast_to((128, 1)))
            kb.dma(sync, tmpw[:, :ET * DTR], wd1T_c)
        ld0 = kb.cnt["L"]

        @B.scalar
        def _(scalar):
            scalar = kb.wrap(scalar, "S")
            kb.wait(scalar, "L", ld0)
            kb.inc(scalar.activation(out=A_sb[:], in_=A_sb[:], func=AF.Exp), "S")
        s_a = kb.cnt["S"]

        @B.vector
        def _(vector):
            vector = kb.wrap(vector, "V")
            kb.wait(vector, "L", ld0)
            vector.memset(zero1[:], 0.0)
            vector.memset(halo_sb[:], 0.0)
            vector.memset(h0_sb[:], 0.0)
            vector.tensor_copy(wd1T_sb[:], tmpw[:, :ET * DTR])
            kb.wait(vector, "S", s_a)
            vector.tensor_copy(nA_sb[:], A_sb[:])
            kb.inc(vector.tensor_scalar(A_sb[:], A_sb[:], -1.0, None, ALU.mult), "V")

        @B.sync
        def _(sync):
            kb.wait(sync, "V")
            kb.dma(sync, tmpw[:, :ET * NS], wbT_c)
            kb.dma(sync, tmpw[:, ET * NS:2 * ET * NS], wcT_c)
        ld1 = kb.cnt["L"]

        @B.vector
        def _(vector):
            vector = kb.wrap(vector, "V")
            kb.wait(vector, "L", ld1)
            vector.tensor_copy(wbT_sb[:], tmpw[:, :ET * NS])
            kb.inc(vector.tensor_copy(wcT_sb[:], tmpw[:, ET * NS:2 * ET * NS]), "V")

        @B.sync
        def _(sync):
            kb.wait(sync, "V")
            kb.dma(sync, tmpw[:DTR, :E], wd2T)
        ld2 = kb.cnt["L"]

        @B.vector
        def _(vector):
            vector = kb.wrap(vector, "V")
            kb.wait(vector, "L", ld2)
            kb.inc(vector.tensor_copy(wd2T_sb[:], tmpw[:DTR, :E]), "V")

    dbgmap = {"sdelta": (sdelta, F32), "sxc": (sxc, F32), "sskip": (sskip, F32),
              "sym": (sym, BF16), "sypart": (sypart, F32)}
    dbg_out = None
    if dbg is not None:
        dbg_out = nc.dram_tensor("dbg", [E, T], dbgmap[dbg][1], kind="ExternalOutput").ap()

    def finish():
        @B.sync
        def _(sync):
            kb.wait(sync, "St")
            kb.wait(sync, "V")
            kb.wait(sync, "S")
            kb.wait(sync, "Tn")
            kb.dma(sync, out, racc, sem="St")
            if dbg is not None:
                kb.dma(sync, dbg_out, dbgmap[dbg][0], sem="St")
            kb.wait(sync, "St")
        B.__exit__(None, None, None)
        return nc, kb

    if upto == "p0":
        return finish()

    def drive(*gens):
        live = [g for g, _ in gens]
        reps = {id(g): r for g, r in gens}
        while live:
            for g in list(live):
                for _ in range(reps[id(g)]):
                    try:
                        next(g)
                    except StopIteration:
                        live.remove(g)
                        break


    def gen_p0b():
      with contextlib.ExitStack() as st:
        qb0 = quant_bufs(st, nc, "q0", I)
        wtile = [st.enter_context(nc.sbuf_tensor(f"q0_w{j}", [128, I], F32)) for j in range(2)]
        wqbf = [st.enter_context(nc.sbuf_tensor(f"q0_wq{j}", [128, I], BF16)) for j in range(2)]
        wide = st.enter_context(nc.sbuf_tensor("q0_wide", [128, 22 * 128], BF16))
        qps = [st.enter_context(nc.psum_tensor(f"q0_ps{j}", [128, 128], BF16)) for j in range(2)]

        jobs = []
        for it in range(I // 128):
            jobs.append((wg, swgT, it, D))
        for it in range(I // 128):
            jobs.append((wu, swuT, it, D))
        for it in range(D // 128):
            jobs.append((wdn, swdT, it, I))

        q_done = [kb.cnt["V"], kb.cnt["V"]]
        tr_done = [kb.cnt["Tn"], kb.cnt["Tn"]]
        ld_done = [0, 0]
        st_done = kb.cnt["Stg"]
        for ji, (srcw, dst, it, cols) in enumerate(jobs):
            j = ji % 2
            lsem = "Lg"
            @B.gpsimd
            def _(gpsimd):
                kb.wait(gpsimd, "V", q_done[j])
                kb.dma(gpsimd, wtile[j][:, :cols], srcw[it * 128:(it + 1) * 128, :], sem=lsem)
            ld_done[j] = kb.cnt[lsem]

            @B.vector
            def _(vector):
                vector = kb.wrap(vector, "V")
                kb.wait(vector, lsem, ld_done[j])
                kb.wait(vector, "Tn", tr_done[j])
                emit_quant(kb, vector, wtile[j][:, :cols], wqbf[j][:, :cols], qb0, cols)
            q_done[j] = kb.cnt["V"]

            nblk = cols // 128
            sba = SBatch(kb)
            ev = 0
            for jt in range(nblk):
                pj = jt % 2
                @B.tensor
                def _(tensor):
                    kb.wait(tensor, "V", q_done[j])
                    kb.wait(tensor, "S")
                    kb.inc(tensor.transpose(qps[pj][:], wqbf[j][:, jt * 128:(jt + 1) * 128], eye_sb[:]), "Tn")
                tj = kb.cnt["Tn"]
                sba.wait("Tn", tj)
                if jt == 0:
                    sba.wait("Stg", st_done)
                ev = sba.act(wide[:, jt * 128:(jt + 1) * 128], qps[pj][:], AF.Copy)
            sba.flush(B)
            tr_done[j] = kb.cnt["Tn"]

            @B.scalar
            def _(scalar):
                kb.wait(scalar, "S", ev)
                kb.dma(scalar, dst.rearrange("(a p) k -> p a k", p=128)[:, :, it * 128:(it + 1) * 128],
                       wide[:].rearrange("p (a k) -> p a k", k=128)[:, :nblk, :], sem="Stg")
            st_done = kb.cnt["Stg"]
            yield

    if upto == "p0b":
        return finish()

    def gen_p1(xsrc, xrT_dst, own, xqT_dst):
        with contextlib.ExitStack() as st:
            sb = lambda n, s, dt=F32: st.enter_context(nc.sbuf_tensor(n, s, dt))
            pre = "p1o" if own else "p1p"
            xb = sb(pre + "_x", [128, D])
            xnb = sb(pre + "_xn", [128, D])
            scr = sb(pre + "_scr", [128, D])
            zb = sb(pre + "_zb", [128, D], BF16)
            trp = [st.enter_context(nc.psum_tensor(pre + f"_ps{j}", [128, 128], BF16)) for j in range(2)]
            if own:
                rb = sb(pre + "_r", [128, D])
                qbf = sb(pre + "_qbf", [128, D], BF16)
                qb1 = quant_bufs(st, nc, pre + "q", D)

            for tt in range(min(TT, LIM_TT)):
                sba = SBatch(kb)
                @B.sync
                def _(sync):
                    kb.wait(sync, "St")
                    kb.wait(sync, "V")
                    kb.wait(sync, "S")
                    kb.dma(sync, xb[:], xsrc[tt * 128:(tt + 1) * 128, :])
                ldv = kb.cnt["L"]

                sba.wait("L", ldv)
                s_sq = sba.act(scr[:], xb[:], AF.Square, accum_out=stat[:, 0:1])

                @B.vector
                def _(vector):
                    vector = kb.wrap(vector, "V")
                    kb.wait(vector, "L", ldv)
                    vector.tensor_reduce(stat[:, 1:2], xb[:], axis=AX.X, op=ALU.add)
                    vector.tensor_scalar(stat[:, 2:3], stat[:, 1:2], 1.0 / D, None, ALU.mult)
                    kb.wait(vector, "S", s_sq)
                    vector.tensor_scalar(stat[:, 4:5], stat[:, 0:1], 1.0 / D, 1e-5, ALU.mult, ALU.add)
                    vector.tensor_tensor(stat[:, 5:6], stat[:, 2:3], stat[:, 2:3], ALU.mult)
                    vector.tensor_tensor(stat[:, 4:5], stat[:, 4:5], stat[:, 5:6], ALU.subtract)
                v_rv = kb.cnt["V"]

                sba.wait("V", v_rv)
                sba.act(stat[:, 5:6], stat[:, 4:5], AF.Ln)
                s_rstd = sba.act(stat[:, 6:7], stat[:, 5:6], AF.Exp, scale=-0.5)

                @B.vector
                def _(vector):
                    vector = kb.wrap(vector, "V")
                    kb.wait(vector, "S", s_rstd)
                    vector.tensor_scalar(xnb[:], xb[:], stat[:, 2:3], stat[:, 6:7],
                                         ALU.subtract, ALU.mult)
                v_xn = kb.cnt["V"]

                if own:
                    @B.vector
                    def _(vector):
                        vector = kb.wrap(vector, "V")
                        vector.tensor_tensor(rb[:], xb[:], xnb[:], ALU.add)
                    v_r = kb.cnt["V"]

                    @B.sync
                    def _(sync):
                        kb.wait(sync, "V", v_r)
                        kb.dma(sync, racc[tt * 128:(tt + 1) * 128, :], rb[:], sem="St")

                    @B.vector
                    def _(vector):
                        vector = kb.wrap(vector, "V")
                        emit_quant(kb, vector, xnb[:], qbf[:], qb1, D)
                    v_q = kb.cnt["V"]

                    for dt in range(8):
                        @B.tensor
                        def _(tensor):
                            kb.wait(tensor, "V")
                            kb.inc(tensor.transpose(trp[dt % 2][:], qbf[:, dt * 128:(dt + 1) * 128], eye_sb[:]), "Tn")

                        @B.vector
                        def _(vector):
                            vector = kb.wrap(vector, "V")
                            kb.wait(vector, "Tn")
                            vector.tensor_copy(xqT_dst[:, dt, tt * 128:(tt + 1) * 128], trp[dt % 2][:])

                sba.wait("V", v_xn)
                s_sq2 = sba.act(scr[:], xnb[:], AF.Square, accum_out=stat[:, 0:1])

                @B.vector
                def _(vector):
                    vector = kb.wrap(vector, "V")
                    kb.wait(vector, "S", s_sq2)
                    vector.tensor_scalar(stat[:, 4:5], stat[:, 0:1], 1.0 / D, 1e-5, ALU.mult, ALU.add)
                v_rr = kb.cnt["V"]

                sba.wait("V", v_rr)
                sba.act(stat[:, 5:6], stat[:, 4:5], AF.Ln)
                s_rms = sba.act(stat[:, 6:7], stat[:, 5:6], AF.Exp, scale=-0.5)

                @B.vector
                def _(vector):
                    vector = kb.wrap(vector, "V")
                    kb.wait(vector, "S", s_rms)
                    vector.tensor_scalar(zb[:], xnb[:], stat[:, 6:7], None, ALU.mult)
                v_zb = kb.cnt["V"]

                tr_done = []
                for dt in range(8):
                    @B.tensor
                    def _(tensor):
                        kb.wait(tensor, "V", kb.cnt["V"])
                        kb.wait(tensor, "S")
                        kb.inc(tensor.transpose(trp[dt % 2][:], zb[:, dt * 128:(dt + 1) * 128], eye_sb[:]), "Tn")
                    tr_done.append(kb.cnt["Tn"])
                    sba.wait("Tn", tr_done[-1])
                    sba.act(xrT_dst[:, dt, tt * 128:(tt + 1) * 128], trp[dt % 2][:],
                            AF.Copy, scale=normw_sb[:, dt:dt + 1])
                sba.flush(B)
                yield
    xqT_sb = st_xrB.enter_context(nc.sbuf_tensor("xqT_sb", [128, 8, T], BF16))
    drive((gen_p0b(), 3), (gen_p1(x_own, xrT_own, True, xqT_sb), 1))

    if upto == "p1":
        return finish()

    with contextlib.ExitStack() as st:
        sb = lambda n, s, dt=F32: st.enter_context(nc.sbuf_tensor(n, s, dt))
        IH = 1408
        NIT = IH // 128
        wgh = sb("p2_wgh", [128, 8, IH], BF16)
        wuh = sb("p2_wuh", [128, 8, IH], BF16)
        wdh = sb("p2_wdh", [128, NIT, D], BF16)
        hbuf = sb("p2_h", [128, 512])
        hb2 = sb("p2_h2", [128, 512])
        hq = [sb(f"p2_hq{j}", [128, 512], BF16) for j in range(2)]
        hqT = [sb(f"p2_hqT{j}", [128, 4, 128], BF16) for j in range(2)]
        fbuf = sb("p2_f", [128, D])
        f0buf = sb("p2_f0", [128, D])
        rbuf2 = sb("p2_r", [128, D])
        qb2 = quant_bufs(st, nc, "p2q", 512)
        ps_g = [st.enter_context(nc.psum_tensor(f"p2_psg{j}", [128, 512], F32)) for j in range(2)]
        ps_u = [st.enter_context(nc.psum_tensor(f"p2_psu{j}", [128, 512], F32)) for j in range(2)]
        ps_d0 = st.enter_context(nc.psum_tensor("p2_psd0", [128, 512], F32))
        ps_d1 = st.enter_context(nc.psum_tensor("p2_psd1", [128, 512], F32))
        ps_t = [st.enter_context(nc.psum_tensor(f"p2_pst{j}", [128, 128], BF16)) for j in range(2)]

        segs = [(0, 512), (512, 512), (1024, 384)]
        NSEG = len(segs)
        for ih in range(2):
            i0 = ih * IH
            @B.sync
            def _(sync):
                kb.wait(sync, "Tn")
                kb.wait(sync, "St")
                kb.wait(sync, "Stg")
                kb.dma(sync, wgh[:], swgT.rearrange("(a p) i -> p a i", p=128)[:, :, i0:i0 + IH])
                kb.dma(sync, wuh[:], swuT.rearrange("(a p) i -> p a i", p=128)[:, :, i0:i0 + IH])
                kb.dma(sync, wdh[:], swdT.rearrange("(a p) d -> p a d", p=128)[:, NIT * ih:NIT * (ih + 1), :])
            ldw = kb.cnt["L"]

            for tt in range(min(TT, LIM_TT)):
                sba = SBatch(kb)
                t_gu = [0] * NSEG
                gu_free = [kb.cnt["S"], kb.cnt["S"]]
                gu_free_v = [kb.cnt["V"], kb.cnt["V"]]
                hq_free = [kb.cnt["Tn"], kb.cnt["Tn"]]
                hqT_free = [kb.cnt["Tn"], kb.cnt["Tn"]]
                ev_done = [0] * NSEG
                for si in range(NSEG + 1):
                    if si < NSEG:
                        s0, sl = segs[si]
                        jsi = si % 2

                        @B.tensor
                        def _(tensor):
                            kb.wait(tensor, "L", ldw)
                            kb.wait(tensor, "S", gu_free[jsi])
                            kb.wait(tensor, "V", gu_free_v[jsi])
                            for kd in range(8):
                                tensor.matmul(ps_g[jsi][:, :sl],
                                              xqT_sb[:, kd, tt * 128:(tt + 1) * 128],
                                              wgh[:, kd, s0:s0 + sl],
                                              start=(kd == 0), stop=(kd == 7))
                            for kd in range(8):
                                mi = tensor.matmul(ps_u[jsi][:, :sl],
                                                   xqT_sb[:, kd, tt * 128:(tt + 1) * 128],
                                                   wuh[:, kd, s0:s0 + sl],
                                                   start=(kd == 0), stop=(kd == 7))
                            kb.inc(mi, "Tn")
                        t_gu[si] = kb.cnt["Tn"]

                    if si >= 1:
                        sp = si - 1
                        s0, sl = segs[sp]
                        jsp = sp % 2

                        sba.wait("Tn", t_gu[sp])
                        sba.wait("V", gu_free_v[1 - jsp])
                        s_h = sba.act(hbuf[:, :sl], ps_g[jsp][:, :sl], AF.Silu)
                        gu_free[jsp] = s_h

                        @B.vector
                        def _(vector):
                            vector = kb.wrap(vector, "V")
                            kb.wait(vector, "S", s_h)
                            kb.wait(vector, "Tn", hq_free[jsp])
                            vector.tensor_tensor(hb2[:, :sl], hbuf[:, :sl], ps_u[jsp][:, :sl], ALU.mult)
                            emit_quant(kb, vector, hb2[:, :sl], hq[jsp][:, :sl], qb2, sl)
                        gu_free_v[jsp] = kb.cnt["V"]
                        v_hq = kb.cnt["V"]

                        nsub = sl // 128
                        evs = 0
                        for isub in range(nsub):
                            pj = isub % 2
                            @B.tensor
                            def _(tensor):
                                kb.wait(tensor, "V", v_hq)
                                kb.wait(tensor, "S")
                                kb.inc(tensor.transpose(ps_t[pj][:], hq[jsp][:, isub * 128:(isub + 1) * 128], eye_sb[:]), "Tn")
                            tj = kb.cnt["Tn"]

                            sba.wait("Tn", max(tj, hqT_free[jsp]))
                            evs = sba.act(hqT[jsp][:, isub, :], ps_t[pj][:], AF.Copy)
                        hq_free[jsp] = kb.cnt["Tn"]

                        @B.tensor
                        def _(tensor):
                            kb.wait(tensor, "S", evs)
                            it_base = s0 // 128
                            for isub in range(nsub):
                                it = it_base + isub
                                first = (sp == 0 and isub == 0)
                                last = (sp == NSEG - 1 and isub == nsub - 1)
                                tensor.matmul(ps_d0[:], hqT[jsp][:, isub, :], wdh[:, it, 0:512],
                                              start=first, stop=last)
                                mi = tensor.matmul(ps_d1[:], hqT[jsp][:, isub, :], wdh[:, it, 512:1024],
                                                   start=first, stop=last)
                            kb.inc(mi, "Tn")
                        hqT_free[jsp] = kb.cnt["Tn"]
                t_d = kb.cnt["Tn"]

                if ih == 0:
                    sba.wait("Tn", t_d)
                    sba.wait("St")
                    sba.act(fbuf[:, 0:512], ps_d0[:], AF.Copy)
                    s_f = sba.act(fbuf[:, 512:1024], ps_d1[:], AF.Copy)
                    sba.flush(B)

                    @B.sync
                    def _(sync):
                        kb.wait(sync, "S", s_f)
                        kb.dma(sync, sffn[tt * 128:(tt + 1) * 128, :], fbuf[:], sem="St")
                else:
                    @B.sync
                    def _(sync):
                        kb.wait(sync, "V")
                        kb.wait(sync, "St")
                        kb.dma(sync, f0buf[:], sffn[tt * 128:(tt + 1) * 128, :])
                        kb.dma(sync, rbuf2[:], racc[tt * 128:(tt + 1) * 128, :])
                    ldr = kb.cnt["L"]

                    sba.wait("Tn", t_d)
                    sba.wait("St")
                    sba.act(fbuf[:, 0:512], ps_d0[:], AF.Copy)
                    s_f = sba.act(fbuf[:, 512:1024], ps_d1[:], AF.Copy)
                    sba.flush(B)

                    @B.vector
                    def _(vector):
                        vector = kb.wrap(vector, "V")
                        kb.wait(vector, "S", s_f)
                        kb.wait(vector, "L", ldr)
                        vector.tensor_tensor(fbuf[:], fbuf[:], f0buf[:], ALU.add)
                        vector.tensor_tensor(fbuf[:], fbuf[:], rbuf2[:], ALU.add)
                    v_f = kb.cnt["V"]

                    @B.sync
                    def _(sync):
                        kb.wait(sync, "V", v_f)
                        kb.dma(sync, racc[tt * 128:(tt + 1) * 128, :], fbuf[:], sem="St")

    st_xrB.close()
    st_xrB = contextlib.ExitStack()
    xrT_pre = st_xrB.enter_context(nc.sbuf_tensor("xrT_pre", [128, 8, T], BF16))
    for _ in gen_p1(x_pre, xrT_pre, False, None):
        pass

    def p3_chunks(xrT_src, own):
        with contextlib.ExitStack() as st:
            sb = lambda n, s, dt=F32: st.enter_context(nc.sbuf_tensor(n, s, dt))
            pre = "c3o" if own else "c3p"
            inw_sb = sb(pre + "_inw", [128, 8, E], BF16)
            tmpc = sb(pre + "_tmpc", [128, E])
            xib = [sb(pre + f"_xi{j}", [128, 3 + CH]) for j in range(2)]
            cb = sb(pre + "_cb", [128, CH])
            cb2 = sb(pre + "_cb2", [128, CH])
            cb3 = sb(pre + "_cb3", [128, CH])
            xcb = sb(pre + "_xc", [128, CH])
            xcbf = [sb(pre + f"_xcbf{j}", [128, CH], BF16) for j in range(2)]
            dlowb = sb(pre + "_dlow", [DTR, CH], BF16)
            zb = sb(pre + "_zb", [128, CH])
            db = sb(pre + "_db", [128, CH])
            bcb = sb(pre + "_bcb", [NS, 2 * CH], BF16)
            if own:
                skw_sb = sb(pre + "_skw", [128, 8, E], BF16)
                skb = sb(pre + "_skb", [128, CH])
            ps_x = [st.enter_context(nc.psum_tensor(pre + f"_psx{j}", [128, CH], F32)) for j in range(2)]
            ps_dl = st.enter_context(nc.psum_tensor(pre + "_psdl", [DTR, CH], F32))
            ps_B = st.enter_context(nc.psum_tensor(pre + "_psB", [NS, CH], F32))
            ps_C = st.enter_context(nc.psum_tensor(pre + "_psC", [NS, CH], F32))
            ps_d2 = st.enter_context(nc.psum_tensor(pre + "_psd2", [128, CH], F32))
            if own:
                ps_sk = st.enter_context(nc.psum_tensor(pre + "_pssk", [128, CH], F32))

            for dt_ in range(8):
                @B.sync
                def _(sync):
                    kb.wait(sync, "V")
                    kb.wait(sync, "S")
                    kb.wait(sync, "St")
                    kb.wait(sync, "Tn")
                    kb.dma(sync, tmpc[:], inwT.rearrange("(a p) e -> p a e", p=128)[:, dt_, :])
                ldv = kb.cnt["L"]

                @B.vector
                def _(vector):
                    vector = kb.wrap(vector, "V")
                    kb.wait(vector, "L", ldv)
                    kb.inc(vector.tensor_copy(inw_sb[:, dt_, :], tmpc[:]), "V")
            if own:
                for dt_ in range(8):
                    @B.sync
                    def _(sync):
                        kb.wait(sync, "V")
                        kb.dma(sync, tmpc[:], skipwT.rearrange("(a p) e -> p a e", p=128)[:, dt_, :])
                    ldv = kb.cnt["L"]

                    @B.vector
                    def _(vector):
                        vector = kb.wrap(vector, "V")
                        kb.wait(vector, "L", ldv)
                        kb.inc(vector.tensor_copy(skw_sb[:, dt_, :], tmpc[:]), "V")

            for c in range(NCH):
                sba = SBatch(kb)
                c0 = c * CH
                ev_done = [kb.cnt["S"], kb.cnt["S"]]
                cv_done = [kb.cnt["V"], kb.cnt["V"]]
                sg_done = [kb.cnt["Sg"], kb.cnt["Sg"]]
                for et in range(ET):
                    xcf = xcbf[et % 2]
                    jx = et % 2

                    @B.tensor
                    def _(tensor):
                        kb.wait(tensor, "S", ev_done[jx])
                        kb.wait(tensor, "V")
                        for kd in range(8):
                            mi = tensor.matmul(ps_x[jx][:], inw_sb[:, kd, et * 128:(et + 1) * 128],
                                               xrT_src[:, kd, c0:c0 + CH],
                                               start=(kd == 0), stop=(kd == 7))
                        kb.inc(mi, "Tn")
                    t_x = kb.cnt["Tn"]

                    sba.wait("Tn", t_x)
                    sba.wait("V", cv_done[jx])
                    sba.wait("Sg", sg_done[jx])
                    s_xi = sba.act(xib[jx][:, 3:3 + CH], ps_x[jx][:], AF.Copy)
                    ev_done[jx] = s_xi

                    @B.vector
                    def _(vector):
                        vector = kb.wrap(vector, "V")
                        kb.wait(vector, "S", s_xi)
                        vector.tensor_copy(xib[jx][:, 0:3], halo_sb[:, et * 3:(et + 1) * 3])
                        vector.tensor_copy(halo_sb[:, et * 3:(et + 1) * 3], xib[jx][:, CH:CH + 3])
                    v_halo = kb.cnt["V"]

                    @B.vector
                    def _(vector):
                        vector = kb.wrap(vector, "V")
                        kb.wait(vector, "S", kb.cnt["S"])
                        vector.tensor_scalar(cb[:], xib[jx][:, 0:CH], convw_sb[:, et * 4:et * 4 + 1],
                                             None, ALU.mult)
                        vector.scalar_tensor_tensor(cb2[:], xib[jx][:, 1:1 + CH],
                                                    convw_sb[:, et * 4 + 1:et * 4 + 2], cb[:],
                                                    ALU.mult, ALU.add)
                        vector.scalar_tensor_tensor(cb[:], xib[jx][:, 2:2 + CH],
                                                    convw_sb[:, et * 4 + 2:et * 4 + 3], cb2[:],
                                                    ALU.mult, ALU.add)
                        vector.scalar_tensor_tensor(cb2[:], xib[jx][:, 3:3 + CH],
                                                    convw_sb[:, et * 4 + 3:et * 4 + 4], cb[:],
                                                    ALU.mult, ALU.add)
                    v_cv = kb.cnt["V"]
                    cv_done[jx] = v_cv
                    sg_done[jx] = kb.cnt["Sg"]

                    sba.wait("V", v_cv)
                    sba.wait("St")
                    s_xc = sba.act(xcb[:], cb2[:], AF.Silu, bias=convb_sb[:, et:et + 1])

                    @B.vector
                    def _(vector):
                        vector = kb.wrap(vector, "V")
                        kb.wait(vector, "S", s_xc)
                        kb.inc(vector.tensor_copy(xcf[:], xcb[:]), "V")
                    v_xcf = kb.cnt["V"]

                    @B.sync
                    def _(sync):
                        kb.wait(sync, "S", s_xc)
                        kb.dma(sync, sxc[et * 128:(et + 1) * 128, c0:c0 + CH], xcb[:], sem="St")

                    @B.tensor
                    def _(tensor):
                        kb.wait(tensor, "V", v_xcf)
                        tensor.matmul(ps_dl[:], wd1T_sb[:, et * DTR:(et + 1) * DTR], xcf[:],
                                      start=(et == 0), stop=(et == ET - 1))
                        tensor.matmul(ps_B[:], wbT_sb[:, et * NS:(et + 1) * NS], xcf[:],
                                      start=(et == 0), stop=(et == ET - 1))
                        mi = tensor.matmul(ps_C[:], wcT_sb[:, et * NS:(et + 1) * NS], xcf[:],
                                           start=(et == 0), stop=(et == ET - 1))
                        if own:
                            for kd in range(8):
                                mi = tensor.matmul(ps_sk[:], skw_sb[:, kd, et * 128:(et + 1) * 128],
                                                   xrT_src[:, kd, c0:c0 + CH],
                                                   start=(kd == 0), stop=(kd == 7))
                        kb.inc(mi, "Tn")
                    t_bc = kb.cnt["Tn"]

                    if own:
                        sba.wait("Tn", t_bc)
                        sba.wait("St")
                        s_sk = sba.act(skb[:], ps_sk[:], AF.Silu)

                        @B.sync
                        def _(sync):
                            kb.wait(sync, "S", s_sk)
                            kb.dma(sync, sskip[et * 128:(et + 1) * 128, c0:c0 + CH], skb[:], sem="St")

                t_dl = kb.cnt["Tn"]
                sba.flush(B)
                sba = SBatch(kb)

                sba.wait("Tn", t_dl)
                sba.act(dlowb[:], ps_dl[:], AF.Copy)
                sba.act(bcb[:, 0:CH], ps_B[:], AF.Copy)
                s_dl = sba.act(bcb[:, CH:2 * CH], ps_C[:], AF.Copy)

                @B.sync
                def _(sync):
                    kb.wait(sync, "S", s_dl)
                    kb.dma(sync, sB[:, c0:c0 + CH], bcb[:, 0:CH], sem="St")
                    kb.dma(sync, sC[:, c0:c0 + CH], bcb[:, CH:2 * CH], sem="St")

                for et in range(ET):
                    @B.tensor
                    def _(tensor):
                        kb.wait(tensor, "S", s_dl)
                        kb.wait(tensor, "V")
                        kb.inc(tensor.matmul(ps_d2[:], wd2T_sb[:, et * 128:(et + 1) * 128],
                                             dlowb[:], start=True, stop=True), "Tn")
                    t_d2 = kb.cnt["Tn"]

                    @B.vector
                    def _(vector):
                        vector = kb.wrap(vector, "V")
                        kb.wait(vector, "Tn", t_d2)
                        kb.wait(vector, "S")
                        kb.inc(vector.tensor_scalar(zb[:], ps_d2[:], wd2b_sb[:, et:et + 1],
                                                    80.0, ALU.add, ALU.min), "V")
                    v_zb2 = kb.cnt["V"]

                    sba.wait("V", v_zb2)
                    sba.wait("St")
                    sba.act(db[:], zb[:], AF.Exp)
                    s_db = sba.act(db[:], db[:], AF.Ln, bias=1.0)

                    @B.sync
                    def _(sync):
                        kb.wait(sync, "S", s_db)
                        kb.dma(sync, sdelta[et * 128:(et + 1) * 128, c0:c0 + CH], db[:], sem="St")
                sba.flush(B)

    if upto == "p1pre":
        return finish()

    p3_chunks(xrT_pre, False)
    st_xrB.close()
    if upto == "p3pre":
        return finish()

    def p3_scan(own):
        with contextlib.ExitStack() as st:
            sb = lambda n, s, dt=F32: st.enter_context(nc.sbuf_tensor(n, s, dt))
            pre = "s3o" if own else "s3p"
            Brep = sb(pre + "_Brep", [128, 8, T], BF16)
            dtile = [sb(pre + f"_dt{j}", [128, T]) for j in range(2)]
            xct = [sb(pre + f"_xct{j}", [128, T]) for j in range(2)]
            dxt = [sb(pre + f"_dxt{j}", [128, T]) for j in range(2)]
            ab = [sb(pre + f"_ab{j}", [128, T]) for j in range(2)]
            ub = [sb(pre + f"_ub{j}", [128, T]) for j in range(2)]
            if own:
                Crep = sb(pre + "_Crep", [128, 8, T], BF16)
                hb = [sb(pre + f"_hb{j}", [128, T], BF16) for j in range(2)]
                hcb = [sb(pre + f"_hcb{j}", [128, T], BF16) for j in range(2)]
                ypb = sb(pre + "_ypb", [128, T])
                ytl = sb(pre + "_ytl", [128, T])
                skt = sb(pre + "_skt", [128, T])
                ymb = sb(pre + "_ymb", [128, T], BF16)
                ps_y = [st.enter_context(nc.psum_tensor(pre + f"_psy{j}", [128, 512], F32))
                        for j in range(4)]
                ps_ht = st.enter_context(nc.psum_tensor(pre + "_psht", [128, 128], F32))
            else:
                hb = [sb(pre + f"_hb{j}", [128, T]) for j in range(2)]

            ab_free = [kb.cnt["V"], kb.cnt["V"]]
            ub_free = [kb.cnt["V"], kb.cnt["V"]]
            hc_free = [kb.cnt["Tn"], kb.cnt["Tn"]]
            c0s = (kb.cnt["V"], kb.cnt["S"], kb.cnt["Sg"])
            et_cons = [c0s, c0s]
            ld_t = [0, 0]
            TH = T // 2
            for nh in range(2):
                @B.sync
                def _(sync):
                    kb.wait(sync, "V")
                    kb.wait(sync, "S")
                    kb.wait(sync, "St")
                    kb.wait(sync, "Tn")
                    kb.wait(sync, "Sg")
                    kb.dma(sync, Brep[:], sB[nh * 8:(nh + 1) * 8, :].partition_broadcast(128))
                    if own:
                        kb.dma(sync, Crep[:], sC[nh * 8:(nh + 1) * 8, :].partition_broadcast(128))
                    kb.dma(sync, dtile[0][:], sdelta[0:128, :])
                    kb.dma(sync, xct[0][:], sxc[0:128, :])
                ld_t[0] = kb.cnt["L"]

                for et in range(ET):
                    j_et = et % 2
                    ldt = ld_t[j_et]

                    @B.vector
                    def _(vector):
                        vector = kb.wrap(vector, "V")
                        kb.wait(vector, "L", ldt)
                        vector.tensor_tensor(dxt[j_et][:], dtile[j_et][:], xct[j_et][:], ALU.mult)
                    v_dx = kb.cnt["V"]

                    if et + 1 < ET:
                        jn = (et + 1) % 2
                        @B.sync
                        def _(sync):
                            cv, cs, cg = et_cons[jn]
                            kb.wait(sync, "V", cv)
                            kb.wait(sync, "S", cs)
                            kb.wait(sync, "Sg", cg)
                            kb.wait(sync, "L", ld_t[j_et])
                            kb.dma(sync, dtile[jn][:], sdelta[(et + 1) * 128:(et + 2) * 128, :])
                            kb.dma(sync, xct[jn][:], sxc[(et + 1) * 128:(et + 2) * 128, :])
                        ld_t[jn] = kb.cnt["L"]

                    if own and nh == 1:
                        @B.sync
                        def _(sync):
                            kb.wait(sync, "V")
                            kb.wait(sync, "L", ld_t[(et + 1) % 2 if et + 1 < ET else j_et])
                            kb.dma(sync, skt[:], sskip[et * 128:(et + 1) * 128, :])
                            kb.dma(sync, ypb[:], sypart[et * 128:(et + 1) * 128, :])
                        ld_sk = kb.cnt["L"]
                    else:
                        ld_sk = 0

                    sba = SBatch(kb)
                    s_ab = [0] * 8
                    g_u = [0] * 8
                    for n in range(8):
                        j = n % 2
                        na = nh * 8 + n
                        sba.wait("L", ldt)
                        if ab_free[j]:
                            sba.wait("V", ab_free[j])
                        s_ab[n] = sba.act(ab[j][:], dtile[j_et][:], AF.Exp,
                                          scale=A_sb[:, et * NS + na:et * NS + na + 1])

                        @B.vector
                        def _(vector):
                            vector = kb.wrap(vector, "V")
                            if ub_free[j]:
                                kb.wait(vector, "V", ub_free[j])
                            vector.tensor_tensor(ub[j][:], dxt[j_et][:],
                                                 Brep[:, n, :], ALU.mult)
                            kb.wait(vector, "S", s_ab[n])
                            if own and hc_free[j]:
                                kb.wait(vector, "Tn", hc_free[j])
                            init = h0_sb[:, et * NS + na:et * NS + na + 1] if own else zero1[:]
                            vector.tensor_tensor_scan(hb[j][:], ab[j][:], ub[j][:],
                                                      init, ALU.mult, ALU.add)
                            ab_free[j] = ub_free[j] = kb.cnt["V"]
                            if own:
                                vector.tensor_tensor(hcb[j][:], hb[j][:], Crep[:, n, :],
                                                     ALU.mult)
                            else:
                                vector.tensor_copy(h0_sb[:, et * NS + na:et * NS + na + 1],
                                                   hb[j][:, T - 1:T])
                        v_hc = kb.cnt["V"]

                        if own:
                            @B.tensor
                            def _(tensor):
                                for pc in (v_hc - 2, v_hc - 1, v_hc):
                                    kb.wait(tensor, "V", pc)
                                    tensor.matmul(ps_ht[:], eye_sb[:], eye_sb[:, 0:128],
                                                  start=True, stop=True)
                                kb.wait(tensor, "V", v_hc)
                                if n == 0:
                                    kb.wait(tensor, "S")
                                    kb.wait(tensor, "V")
                                for sg in range(4):
                                    mi = tensor.matmul(ps_y[sg][:], eye_sb[:],
                                                       hcb[j][:, sg * 512:(sg + 1) * 512],
                                                       start=(n == 0), stop=(n == 7))
                                kb.inc(mi, "Tn")
                                hc_free[j] = kb.cnt["Tn"]
                    t_y = kb.cnt["Tn"]

                    if own:
                        if nh == 0:
                            sba.wait("Tn", t_y)
                            sba.wait("St")
                            for sg in range(4):
                                s_y = sba.act(ytl[:, sg * 512:(sg + 1) * 512],
                                              ps_y[sg][:], AF.Copy)
                            sba.flush(B)

                            @B.sync
                            def _(sync):
                                kb.wait(sync, "S", s_y)
                                kb.dma(sync, sypart[et * 128:(et + 1) * 128, :], ytl[:], sem="St")
                        else:
                            sba.flush(B)

                            @B.vector
                            def _(vector):
                                vector = kb.wrap(vector, "V")
                                kb.wait(vector, "Tn", t_y)
                                kb.wait(vector, "L", ld_sk)
                                kb.wait(vector, "St")
                                for sg in range(4):
                                    vector.scalar_tensor_tensor(
                                        ytl[:, sg * 512:(sg + 1) * 512],
                                        xct[j_et][:, sg * 512:(sg + 1) * 512],
                                        Wd_sb[:, et:et + 1],
                                        ps_y[sg][:], ALU.mult, ALU.add)
                                vector.tensor_tensor(ytl[:], ytl[:], ypb[:], ALU.add)
                                vector.tensor_tensor(ymb[:], ytl[:], skt[:], ALU.mult)
                            v_ym = kb.cnt["V"]

                            @B.sync
                            def _(sync):
                                kb.wait(sync, "V", v_ym)
                                kb.dma(sync, sym[et * 128:(et + 1) * 128, :], ymb[:], sem="St")
                    sba.flush(B)
                    et_cons[j_et] = (kb.cnt["V"], kb.cnt["S"], kb.cnt["Sg"])

            if not own:
                @B.vector
                def _(vector):
                    vector = kb.wrap(vector, "V")
                    vector.tensor_scalar(h0_sb[:], h0_sb[:], maskv[:, 0:1], None,
                                         ALU.mult)
                    vector.tensor_scalar(halo_sb[:], halo_sb[:], maskv[:, 0:1], None,
                                         ALU.mult)


    def p3_prefix():
        with contextlib.ExitStack() as st:
            sb = lambda n, s, dt=F32: st.enter_context(nc.sbuf_tensor(n, s, dt))
            Brep = sb("pf_Brep", [128, NS, T], BF16)
            dtile = [sb(f"pf_dt{j}", [128, T]) for j in range(2)]
            xct = [sb(f"pf_xct{j}", [128, T]) for j in range(2)]
            dxt = [sb(f"pf_dxt{j}", [128, T]) for j in range(2)]
            Eb = [sb(f"pf_E{j}", [128, T]) for j in range(2)]
            ub = [sb(f"pf_ub{j}", [128, T]) for j in range(2)]
            cbuf = sb("pf_c", [128, T])
            onescol = sb("pf_ones", [128, 1])
            bias2 = sb("pf_bias", [128, NS])
            acc1 = sb("pf_acc", [128, 1])
            ps_hp = st.enter_context(nc.psum_tensor("pf_psht", [128, 128], F32))
            ones = onescol[:].broadcast_to((128, T))

            @B.vector
            def _(vector):
                vector = kb.wrap(vector, "V")
                vector.memset(onescol[:], 1.0)

            E_free = [kb.cnt["V"], kb.cnt["V"]]
            u_free = [kb.cnt["V"], kb.cnt["V"]]
            et_cons = [(kb.cnt["V"], kb.cnt["S"], kb.cnt["Sg"])] * 2
            ld_t = [0, 0]
            TH = T // 2

            @B.sync
            def _(sync):
                kb.wait(sync, "V")
                kb.wait(sync, "S")
                kb.wait(sync, "St")
                kb.wait(sync, "Tn")
                kb.wait(sync, "Sg")
                kb.dma(sync, Brep[:], sB.partition_broadcast(128))
                kb.dma(sync, dtile[0][:], sdelta[0:128, :])
                kb.dma(sync, xct[0][:], sxc[0:128, :])
            ld_t[0] = kb.cnt["L"]

            for et in range(ET):
                j_et = et % 2
                ldt = ld_t[j_et]

                @B.vector
                def _(vector):
                    vector = kb.wrap(vector, "V")
                    kb.wait(vector, "L", ldt)
                    vector.tensor_tensor(dxt[j_et][:], dtile[j_et][:], xct[j_et][:], ALU.mult)
                    vector.tensor_tensor_scan(cbuf[:], ones, dtile[j_et][:],
                                              zero1[:], ALU.mult, ALU.add)
                v_dx = kb.cnt["V"]

                if et + 1 < ET:
                    jn = (et + 1) % 2
                    @B.sync
                    def _(sync):
                        cv, cs, cg = et_cons[jn]
                        kb.wait(sync, "V", cv)
                        kb.wait(sync, "Sg", cg)
                        kb.dma(sync, dtile[jn][:], sdelta[(et + 1) * 128:(et + 2) * 128, :])
                        kb.dma(sync, xct[jn][:], sxc[(et + 1) * 128:(et + 2) * 128, :])
                    ld_t[jn] = kb.cnt["L"]

                sba = SBatch(kb)
                s_E = [0] * NS
                g_u = [0] * NS
                v_bias = [0] * NS
                for n in range(NS):
                    j = n % 2
                    col = et * NS + n

                    @B.vector
                    def _(vector):
                        vector = kb.wrap(vector, "V")
                        vector.tensor_tensor(bias2[:, n:n + 1], A_sb[:, col:col + 1],
                                             cbuf[:, T - 1:T], ALU.mult)
                    v_bias[n] = kb.cnt["V"]

                for n in range(NS):
                    j = n % 2
                    col = et * NS + n
                    sba.wait("V", v_bias[n])
                    if E_free[j]:
                        sba.wait("V", E_free[j])
                    s_E[n] = sba.act(Eb[j][:], cbuf[:], AF.Exp,
                                     scale=nA_sb[:, col:col + 1],
                                     bias=bias2[:, n:n + 1])

                    @B.gpsimd
                    def _(gpsimd):
                        gpsimd = kb.wrap(gpsimd, "Sg")
                        kb.wait(gpsimd, "V", v_dx)
                        if u_free[j]:
                            kb.wait(gpsimd, "V", u_free[j])
                        gpsimd.tensor_tensor(ub[j][:, 0:1536], dxt[j_et][:, 0:1536],
                                             Brep[:, n, 0:1536], ALU.mult)
                    g_u[n] = kb.cnt["Sg"]

                    pass

                    @B.vector
                    def _(vector):
                        vector = kb.wrap(vector, "V")
                        if u_free[j]:
                            kb.wait(vector, "V", u_free[j])
                        vector.tensor_tensor(ub[j][:, 1536:T], dxt[j_et][:, 1536:T],
                                             Brep[:, n, 1536:T], ALU.mult)
                        kb.wait(vector, "S", s_E[n])
                        kb.wait(vector, "Sg", g_u[n])
                        vector.scalar_tensor_tensor(Eb[j][:], Eb[j][:], 1.0, ub[j][:],
                                                    ALU.mult, ALU.mult, accum_out=acc1[:])
                        E_free[j] = u_free[j] = kb.cnt["V"]
                        vector.tensor_copy(h0_sb[:, col:col + 1], acc1[:])
                    v_pf = kb.cnt["V"]

                    @B.tensor
                    def _(tensor):
                        for pc in (v_pf - 2, v_pf - 1, v_pf):
                            kb.wait(tensor, "V", pc)
                            tensor.matmul(ps_hp[:], eye_sb[:], eye_sb[:, 0:128],
                                          start=True, stop=True)
                        kb.inc(tensor.nop() if False else tensor.matmul(
                            ps_hp[:], eye_sb[:], eye_sb[:, 0:128], start=True, stop=True), "Tn")
                sba.flush(B)

                et_cons[j_et] = (kb.cnt["V"], kb.cnt["S"], kb.cnt["Sg"])

            @B.vector
            def _(vector):
                vector = kb.wrap(vector, "V")
                vector.tensor_scalar(h0_sb[:], h0_sb[:], maskv[:, 0:1], None, ALU.mult)
                vector.tensor_scalar(halo_sb[:], halo_sb[:], maskv[:, 0:1], None, ALU.mult)

    p3_prefix()
    if upto == "p3scanpre":
        return finish()


    p3_chunks(xrT_own, True)
    st_xrA.close()
    if upto == "p3own":
        return finish()
    p3_scan(True)
    if upto == "p3scanown":
        return finish()

    with contextlib.ExitStack() as st:
        sb = lambda n, s, dt=F32: st.enter_context(nc.sbuf_tensor(n, s, dt))
        oww = sb("c4_oww", [128, ET, D], BF16)
        oww_tmp = [sb(f"c4_owt{j}", [128, D]) for j in range(2)]
        oc_done = [kb.cnt["V"], kb.cnt["V"]]
        for et in range(ET):
            j = et % 2
            @B.gpsimd
            def _(gpsimd):
                kb.wait(gpsimd, "V", oc_done[j])
                kb.dma(gpsimd, oww_tmp[j][:], outwT.rearrange("(a p) d -> p a d", p=128)[:, et, :], sem="Lg")
            ldv = kb.cnt["Lg"]

            @B.vector
            def _(vector):
                vector = kb.wrap(vector, "V")
                kb.wait(vector, "Lg", ldv)
                kb.inc(vector.tensor_copy(oww[:, et, :], oww_tmp[j][:]), "V")
            oc_done[j] = kb.cnt["V"]
        v_oww = kb.cnt["V"]
        ymt = [sb(f"c4_ymt{j}", [128, ET, 128], BF16) for j in range(2)]
        rtile = [sb(f"c4_r{j}", [128, D]) for j in range(2)]
        ob = [sb(f"c4_o{j}", [128, D]) for j in range(2)]
        ps_o = [[st.enter_context(nc.psum_tensor(f"c4_pso{j}_{ds}", [128, 512], F32))
                 for ds in range(2)] for j in range(2)]

        t_os = [0, 0]
        v_os = [0, 0]
        st_ob = [0, 0]
        for tt in range(TT):
            j = tt % 2
            @B.sync
            def _(sync):
                kb.wait(sync, "Tn", t_os[j])
                kb.wait(sync, "V", v_os[j])
                if tt < 2:
                    kb.wait(sync, "St")
                kb.dma(sync, ymt[j][:], sym.rearrange("(a p) t -> p a t", p=128)[:, :, tt * 128:(tt + 1) * 128])
                kb.dma(sync, rtile[j][:], racc[tt * 128:(tt + 1) * 128, :])
            ldy = kb.cnt["L"]

            @B.tensor
            def _(tensor):
                kb.wait(tensor, "L", ldy)
                kb.wait(tensor, "V", v_os[j] if v_os[j] else v_oww)
                for ds in range(2):
                    for et in range(ET):
                        mi = tensor.matmul(ps_o[j][ds][:], ymt[j][:, et, :],
                                           oww[:, et, ds * 512:(ds + 1) * 512],
                                           start=(et == 0), stop=(et == ET - 1))
                kb.inc(mi, "Tn")
            t_os[j] = kb.cnt["Tn"]

            @B.vector
            def _(vector):
                vector = kb.wrap(vector, "V")
                kb.wait(vector, "Tn", t_os[j])
                if st_ob[j]:
                    kb.wait(vector, "St", st_ob[j])
                for ds in range(2):
                    vi = vector.tensor_tensor(ob[j][:, ds * 512:(ds + 1) * 512], ps_o[j][ds][:],
                                              rtile[j][:, ds * 512:(ds + 1) * 512], ALU.add)
                kb.inc(vi, "V")
            v_os[j] = kb.cnt["V"]

            @B.sync
            def _(sync):
                kb.wait(sync, "V", v_os[j])
                kb.dma(sync, out[tt * 128:(tt + 1) * 128, :], ob[j][:], sem="St")
            st_ob[j] = kb.cnt["St"]

    @B.sync
    def _(sync):
        kb.wait(sync, "St")
        if dbg is not None:
            kb.dma(sync, dbg_out, dbgmap[dbg][0], sem="St")
            kb.wait(sync, "St")

    B.__exit__(None, None, None)
    return nc, kb


def host_inputs(resid, weights, core):
    b, h = core // 2, core % 2
    w = weights
    m = {
        "x_own": np.ascontiguousarray(resid[b, h * T:(h + 1) * T]),
        "x_pre": np.ascontiguousarray(resid[b, :T]),
        "maskin": np.array([[float(h)]], np.float32),
        "normw_c": np.ascontiguousarray(w["norm_w"].reshape(8, 128).T),
        "inwT": np.ascontiguousarray(w["in_w"].T),
        "skipwT": np.ascontiguousarray(w["skip_w"].T),
        "outwT": np.ascontiguousarray(w["out_w"].T),
        "wd1T_c": np.ascontiguousarray(w["wd1"].T.reshape(ET, 128, DTR).transpose(1, 0, 2).reshape(128, ET * DTR)),
        "wd2T": np.ascontiguousarray(w["wd2"].T),
        "wbT_c": np.ascontiguousarray(w["wb"].T.reshape(ET, 128, NS).transpose(1, 0, 2).reshape(128, ET * NS)),
        "wcT_c": np.ascontiguousarray(w["wc"].T.reshape(ET, 128, NS).transpose(1, 0, 2).reshape(128, ET * NS)),
        "wd2b_c": np.ascontiguousarray(w["wd2_b"].reshape(ET, 128).T),
        "convb_c": np.ascontiguousarray(w["conv_b"].reshape(ET, 128).T),
        "convw_c": np.ascontiguousarray(w["conv_w"].reshape(ET, 128, 4).transpose(1, 0, 2).reshape(128, ET * 4)),
        "Alog_c": np.ascontiguousarray(w["A_log"].reshape(ET, 128, NS).transpose(1, 0, 2).reshape(128, ET * NS)),
        "Wd_c": np.ascontiguousarray(w["W_D"].reshape(ET, 128).T),
        "wg": np.ascontiguousarray(w["w_gate"]),
        "wu": np.ascontiguousarray(w["w_up"]),
        "wdn": np.ascontiguousarray(w["w_down"]),
        "eye_in": np.eye(128, dtype=np.float32).astype(ml_dtypes.bfloat16),
    }
    return m


def kernel(L0_attn_resid, L0_attn_norm_w, L0_attn_skip_w, L0_attn_in_w,
           L0_attn_conv_w, L0_attn_conv_b, L0_attn_wd1, L0_attn_wd2,
           L0_attn_wd2_b, L0_attn_wb, L0_attn_wc, L0_attn_A_log, L0_attn_W_D,
           L0_attn_out_w, L0_ffn_w_gate, L0_ffn_w_up, L0_ffn_w_down):
    from concourse.bass_utils import run_bass_kernel_spmd
    resid = np.ascontiguousarray(np.asarray(L0_attn_resid, dtype=np.float32))
    w = {
        "norm_w": np.asarray(L0_attn_norm_w, np.float32),
        "skip_w": np.asarray(L0_attn_skip_w, np.float32),
        "in_w": np.asarray(L0_attn_in_w, np.float32),
        "conv_w": np.asarray(L0_attn_conv_w, np.float32),
        "conv_b": np.asarray(L0_attn_conv_b, np.float32),
        "wd1": np.asarray(L0_attn_wd1, np.float32),
        "wd2": np.asarray(L0_attn_wd2, np.float32),
        "wd2_b": np.asarray(L0_attn_wd2_b, np.float32),
        "wb": np.asarray(L0_attn_wb, np.float32),
        "wc": np.asarray(L0_attn_wc, np.float32),
        "A_log": np.asarray(L0_attn_A_log, np.float32),
        "W_D": np.asarray(L0_attn_W_D, np.float32),
        "out_w": np.asarray(L0_attn_out_w, np.float32),
        "w_gate": np.asarray(L0_ffn_w_gate, np.float32),
        "w_up": np.asarray(L0_ffn_w_up, np.float32),
        "w_down": np.asarray(L0_ffn_w_down, np.float32),
    }
    nc, kb = build_nc()
    in_maps = [host_inputs(resid, w, c) for c in range(8)]
    res = run_bass_kernel_spmd(nc, in_maps, core_ids=list(range(8)))
    out = np.zeros((4, 4096, 1024), np.float32)
    for c in range(8):
        b, h = c // 2, c % 2
        out[b, h * T:(h + 1) * T] = res.results[c]["out"]
    return out
